# version 12
# baseline (speedup 1.0000x reference)
import numpy as np
from contextlib import ExitStack

import concourse.bass as bass
import concourse.mybir as mybir
from concourse.bass_utils import run_bass_kernel_spmd

F32 = mybir.dt.float32
F16 = mybir.dt.float16
AF = mybir.ActivationFunctionType
OP = mybir.AluOpType

D = 1024
T = 8192
NT = 2048
NBA = 24
NBB = 56
NDIAG = 8
KVB = NBB * 128
DFF = 4096
NEXP = 8
EPS = 1e-5
NEG = -10000.0

ENGS = ("pe", "act", "dve", "sp", "gp")


class Bld:
    def __init__(self):
        self.nc = bass.Bass("TRN2", target_bir_lowering=False)
        self.q = {e: [] for e in ENGS}
        self.cnt = {}
        self.sem = {}
        self.stack = ExitStack()

    def mksem(self, name):
        self.sem[name] = self.stack.enter_context(self.nc.semaphore(name))
        self.cnt[name] = 0

    def emit(self, eng, fn):
        self.q[eng].append(fn)

    def wait(self, eng, name, val):
        if val > 0:
            self.q[eng].append(lambda e, s=self.sem[name], v=val: e.wait_ge(s, v))

    def bump(self, name, k=1):
        self.cnt[name] += k
        return self.cnt[name]

    def mark(self, eng, name):
        v = self.bump(name)
        f = self.q[eng].pop()
        def g(e, f=f, s=self.sem[name]):
            inst = f(e)
            assert inst is not None, "mark() needs the closure to return inst"
            inst.then_inc(s, 1)
        self.q[eng].append(g)
        return v

    BARRIER_ENGS = ("pe", "act", "dve", "sp")

    def barrier(self):
        tgt = self.cnt["sBAR"] + len(self.BARRIER_ENGS)
        for eng in self.BARRIER_ENGS:
            def f(e, s=self.sem["sBAR"], t=tgt, eng=eng):
                if eng == "dve":
                    e.memset(self.barsc[:], 0.0).then_inc(s, 1)
                elif eng == "act":
                    e.activation(self.barsc[:], self.barsc[:],
                                 AF.Square, bias=0.0,
                                 scale=1.0).then_inc(s, 1)
                else:
                    e.nop().then_inc(s, 1)
                e.wait_ge(s, t)
            self.q[eng].append(f)
        self.cnt["sBAR"] = tgt

    def sbuf(self, name, shape, dt):
        return self.stack.enter_context(self.nc.sbuf_tensor(name, shape, dt))

    def dvechain(self, ops, inc_last=True):
        vals = [self.bump("sDVE") for _ in range(len(ops) - 1)]
        last = self.bump("sDVE") if inc_last else None
        def g(e, ops=ops, vals=vals, s=self.sem["sDVE"]):
            inst = None
            for i, op in enumerate(ops):
                inst = op(e)
                if i < len(ops) - 1:
                    inst.then_inc(s, 1)
                    e.wait_ge(s, vals[i])
            if inc_last:
                inst.then_inc(s, 1)
        self.q["dve"].append(g)
        return self.cnt["sDVE"]


def build_program(debug=False, stop_phase=99, p1a_chunks=7,
                  p1a_mode='full', do_p1b=True):
    b = Bld()
    nc = b.nc

    def din(name, shape, dt):
        return nc.dram_tensor(name, shape, dt, kind="ExternalInput").ap()

    xkvT_d = din("xkvT", [128, 8, T], F16)
    xownT_d = din("xownT", [128, 8, NT], F16)
    wq_d = din("wqT", [128, 8 * D], F16)
    wk_d = din("wkT", [128, 8 * D], F16)
    wv_d = din("wvT", [128, 8 * D], F16)
    wo_d = din("woT", [128, 8 * D], F16)
    bq_d = din("bq", [1, D], F16)
    bk_d = din("bk", [1, D], F16)
    bv_d = din("bv", [1, D], F16)
    bo_d = din("bo", [128, 8], F16)
    brr_d = din("brr", [1, 8], F16)
    amask_d = din("amask", [128, NBA + NBB], F32)
    ids_d = din("ids16", [128, 128], F16)
    ids32_d = din("ids32", [128, 128], F32)
    tri_d = din("tri", [128, 128], F32)
    rank8_d = din("rank8", [128, 8], F32)
    wr32_d = din("wr32", [128, 8, 8], F32)
    b1c_d = din("b1c", [128, 256], F32)
    b2c_d = din("b2c", [128, 256], F32)
    b3_d = din("b3", [8, D], F32)
    e8_d = din("e8sel", [8, 1024], F16)
    w1_d = din("w1", [NEXP, 8, 128, DFF], F32)
    w2_d = din("w2", [NEXP, 8, 128, DFF], F32)
    w3_d = din("w3", [NEXP, 32, 128, D], F32)
    out_d = nc.dram_tensor("y", [128, 8, NT], F32, kind="ExternalOutput").ap()
    if debug:
        dbg_x2 = nc.dram_tensor("dbg_x2", [128, 8 * NT], F16,
                                kind="ExternalOutput").ap()
        dbg_at = nc.dram_tensor("dbg_at", [128, 8 * NT], F16,
                                kind="ExternalOutput").ap()
        dbg_cw = nc.dram_tensor("dbg_cw", [8, NT], F16,
                                kind="ExternalOutput").ap()
        dbg_lt = nc.dram_tensor("dbg_lt", [8, NT], F32,
                                kind="ExternalOutput").ap()
        dbg_tm = nc.dram_tensor("dbg_tm", [128, 256], F32,
                                kind="ExternalOutput").ap()

    w1f = nc.dram_tensor("w1f", [NEXP, 8, 128, DFF], F16, kind="Internal").ap()
    w2f = nc.dram_tensor("w2f", [NEXP, 8, 128, DFF], F16, kind="Internal").ap()
    w3f = nc.dram_tensor("w3f", [NEXP, 32, 128, D], F16, kind="Internal").ap()
    kt_scr = nc.dram_tensor("kt_scr", [128, 8, KVB], F16, kind="Internal").ap()
    vt_scr = nc.dram_tensor("vt_scr", [128, 8, KVB], F16, kind="Internal").ap()
    ktd_scr = nc.dram_tensor("ktd_scr", [128, 8, NT], F16, kind="Internal").ap()
    vtd_scr = nc.dram_tensor("vtd_scr", [128, 8, NT], F16, kind="Internal").ap()

    for s in ("sIN", "sSC", "sW", "sOUT", "sPE", "sACT", "sDVE", "sBAR", "sGP"):
        b.mksem(s)
    SEM = b.sem

    ids = b.sbuf("ids", [128, 128], F16)
    ids32 = b.sbuf("ids32s", [128, 128], F32)
    tri = b.sbuf("tris", [128, 128], F32)
    amask = b.sbuf("amasks", [128, NBA + NBB], F32)
    rank8 = b.sbuf("rank8s", [128, 8], F32)
    bqc = b.sbuf("bqc", [1, D], F16)
    bkc = b.sbuf("bkc", [1, D], F16)
    bvc = b.sbuf("bvc", [1, D], F16)
    boc = b.sbuf("boc", [128, 8], F16)
    brr = b.sbuf("brrs", [1, 8], F16)
    b3s = b.sbuf("b3s", [8, D], F16)
    b1s = b.sbuf("b1s", [128, 256], F32)
    b2s = b.sbuf("b2s", [128, 256], F32)
    e8sel = b.sbuf("e8sels", [8, 1024], F16)
    ones_col = b.sbuf("ones_col", [128, 1], F16)
    b.barsc = b.sbuf("barsc", [128, 1], F16)
    ones_row = b.sbuf("ones_row", [1, 512], F16)
    onesb = b.sbuf("onesb", [128, 64], F16)

    A = [b.sbuf(f"arena{i}", [128, 16384], F16) for i in range(6)]
    A32 = [a[:].bitcast(F32) for a in A]

    pA = b.stack.enter_context(nc.psum_tensor("pA", [128, 2048], F32))
    pB = b.stack.enter_context(nc.psum_tensor("pB", [128, 2048], F32))
    pB16 = pB[:].bitcast(F16)

    xown = A[0][:, 0:16384]
    wqb = A[1][:, 0:8192]
    xcb = [A[3][:, 0:8192], A[3][:, 8192:16384]]
    wkb = A[4][:, 0:8192]
    wvb = A[4][:, 8192:16384]
    stgs = [A[5][:, 0:1024], A[5][:, 1024:2048]]

    attno = A[1][:, 0:16384]
    QT = A[2][:, 0:16384]
    KTb = A[3][:, 0:7168]
    VTb = [A[3][:, 7168:14336], A[4][:, 0:7168]]
    KTd = A[3][:, 14336:16384]
    VTd = A[4][:, 7168:9216]
    at = [A[4][:, 9216:11264], A[4][:, 11264:13312]]
    tmpo = A[4][:, 13312:14336]
    NSLOT = NBB + 2 * NDIAG
    Ve = A[5][:, 4096:4096 + NSLOT * 65]
    Vo = A[5][:, 8776:8776 + NSLOT * 65]
    outs = A32[5][:, 6728:7752]
    drow = A32[5][0:1, 0:1024]
    den16 = A[4][64:65, 14336:15360]

    xown2 = A[0][:, 0:16384]
    x2 = A[2][:, 0:16384]
    wob = A[4][:, 0:8192]
    sq = [A[3][:, 0:2048], A[3][:, 2048:4096]]
    logT = A32[3][0:8, 2048:4096]
    ltm = A32[3][:, 4096:4224]
    ctm = A32[3][:, 4224:4352]
    t2a = A32[3][:, 4352:4368]
    t2b = A32[3][:, 4368:4376]
    t2c = A32[3][:, 4376:4384]
    wrs = A[3][:, 8768:8832]
    rows = A32[5][0:1, 0:3072]
    rows16 = A[5][0:1, 6144:8192]
    cwT = A[5][0:8, 10240:12288]

    def hb(m):
        if m < 16:
            return A[0][:, m * 1024:(m + 1) * 1024]
        return A[1][:, (m - 16) * 1024:(m - 15) * 1024]

    w1b = [A[3][:, 0:4096], A[3][:, 4096:8192]]
    w2b = [A[3][:, 8192:12288], A[3][:, 12288:16384]]
    w3b = [A[4][:, 8192:9216], A[4][:, 9216:10240]]
    s1b = [A[4][:, 10240:11264], A[4][:, 11264:12288]]
    vacc = A[4][:, 0:8192]
    cwb = A[4][:, 12288:13312]
    cwb2 = A[4][:, 14336:15360]
    yst = A32[4][:, 6656:7168]
    ost = [A32[5][:, 6144:7168], A32[5][:, 7168:8192]]

    for dst, src in ((ids[:], ids_d), (ids32[:], ids32_d), (tri[:], tri_d),
                     (amask[:], amask_d), (rank8[:], rank8_d),
                     (bqc[:], bq_d), (bkc[:], bk_d), (bvc[:], bv_d),
                     (boc[:], bo_d), (brr[:], brr_d), (e8sel[:], e8_d),
                     (b1s[:], b1c_d), (b2s[:], b2c_d)):
        def f(e, d=dst, s=src):
            e.dma_start(out=d, in_=s).then_inc(SEM["sIN"], 16)
        b.emit("sp", f)
        b.bump("sIN", 16)
    consts_in = b.cnt["sIN"]
    b.emit("gp", lambda e: e.dma_start(out=b3s[:], in_=b3_d).then_inc(
        SEM["sGP"], 16))
    b.bump("sGP", 16)

    def dve_init(e, v=consts_in):
        e.wait_ge(SEM["sIN"], v)
        e.memset(ones_col[:], 1.0)
        e.memset(onesb[64:65, 0:64], 1.0)
        e.memset(ones_row[:], 1.0).then_inc(SEM["sDVE"], 1)
    b.emit("dve", dve_init)
    b.bump("sDVE", 1)

    def finish():
        return _finish(b, nc)

    if stop_phase <= 0:
        return finish()

    PSL = [pA[:, 0:1024], pA[:, 1024:2048], pB[:, 0:1024], pB[:, 1024:2048]]
    psl_free = [0, 0, 0, 0]
    stage_free = {}

    def proj_unit(u, wbuf, bias, scale, x_ap, dst_sb=None, stage=None,
                  scr_ap=None, pe_waits=()):
        m = u["m"]
        sl = u["slot"] % 4
        b.wait("pe", "sACT", psl_free[sl])
        for (snm, val) in pe_waits:
            b.wait("pe", snm, val)
        ps = PSL[sl]
        for half in range(2):
            for k in range(8):
                def mm(e, o=ps[:, half * 512:(half + 1) * 512],
                       w=wbuf[:, k * D + m * 128:k * D + (m + 1) * 128],
                       x=x_ap(k, half), st=(k == 0)):
                    return e.matmul(o, w, x, start=st, stop=False,
                             skip_group_check=True)
                b.emit("pe", mm)
            def mmb(e, o=ps[:, half * 512:(half + 1) * 512],
                    w=bias[0:1, m * 128:(m + 1) * 128]):
                return e.matmul(o, w, ones_row[0:1, 0:512], start=False, stop=True,
                         skip_group_check=True)
            b.emit("pe", mmb)
        vmm = b.mark("pe", "sPE")
        if p1a_mode == "mm_only":
            psl_free[sl] = 0
            return
        b.wait("act", "sPE", vmm)
        if dst_sb is not None:
            def ev(e, dd=dst_sb, s=ps, sc=scale):
                e.activation(dd, s, AF.Copy, bias=0.0, scale=sc).then_inc(
                    SEM["sACT"], 1)
            b.emit("act", ev)
            psl_free[sl] = b.bump("sACT")
        else:
            sid = id(stage)
            if sid in stage_free:
                b.wait("act", "sSC", stage_free[sid])
            def ev(e, dd=stage, s=ps, sc=scale):
                e.activation(dd, s, AF.Copy, bias=0.0, scale=sc).then_inc(
                    SEM["sACT"], 1)
            b.emit("act", ev)
            psl_free[sl] = b.bump("sACT")
            if p1a_mode == "no_dma":
                return
            b.wait("sp", "sACT", psl_free[sl])
            b.emit("sp", lambda e, s=stage, dd=scr_ap: e.dma_start(
                out=dd, in_=s).then_inc(SEM["sSC"], 16))
            stage_free[sid] = b.bump("sSC", 16)

    for dst, src in ((wkb, wk_d), (wvb, wv_d), (wqb, wq_d)):
        b.emit("sp", lambda e, d=dst, s=src: e.dma_start(
            out=d, in_=s).then_inc(SEM["sIN"], 16))
        b.bump("sIN", 16)
    wload = b.cnt["sIN"]

    b.wait("pe", "sDVE", 1)
    slot = 0
    chunk_pe = {}
    for c in range(p1a_chunks):
        buf = xcb[c % 2]
        if c >= 2:
            b.wait("sp", "sPE", chunk_pe[c - 2])
        b.emit("sp", lambda e, d=buf, cc=c: e.dma_start(
            out=d.rearrange("p (k t) -> p k t", k=8),
            in_=xkvT_d[:, :, cc * 1024:(cc + 1) * 1024]).then_inc(SEM["sIN"], 16))
        cload = b.bump("sIN", 16)
        for m in range(8):
            for tn, (wbuf, bias, scr) in enumerate(
                    ((wkb, bkc, kt_scr), (wvb, bvc, vt_scr))):
                u = {"m": m, "slot": slot}
                proj_unit(
                    u, wbuf, bias, 1.0,
                    lambda k, h, bu=buf: bu[:, k * 1024 + h * 512:
                                            k * 1024 + h * 512 + 512],
                    stage=stgs[slot % 2],
                    scr_ap=scr[:, m, c * 1024:(c + 1) * 1024],
                    pe_waits=(("sIN", max(wload, cload)),))
                slot += 1
        chunk_pe[c] = b.cnt["sPE"]

    if not do_p1b:
        b.barrier()
        return finish()

    b.emit("sp", lambda e: e.dma_start(
        out=xown.rearrange("p (k t) -> p k t", k=8),
        in_=xownT_d).then_inc(SEM["sIN"], 16))
    xo_in = b.bump("sIN", 16)

    for m in range(8):
        for h2 in range(2):
            u = {"m": m, "slot": slot}; slot += 1
            proj_unit(u, wqb, bqc, 0.125,
                      lambda k, h, hh=h2: xown[:, k * NT + hh * 1024 + h * 512:
                                               k * NT + hh * 1024 + h * 512 + 512],
                      dst_sb=QT[:, m * NT + h2 * 1024: m * NT + h2 * 1024 + 1024],
                      pe_waits=(("sIN", xo_in),))
    for jn, (wbuf, bias, scr) in enumerate(((wkb, bkc, ktd_scr),
                                            (wvb, bvc, vtd_scr))):
        for m in range(8):
            for h2 in range(2):
                u = {"m": m, "slot": slot}; slot += 1
                proj_unit(u, wbuf, bias, 1.0,
                          lambda k, h, hh=h2: xown[:, k * NT + hh * 1024 + h * 512:
                                                   k * NT + hh * 1024 + h * 512 + 512],
                          stage=stgs[slot % 2],
                          scr_ap=scr[:, m, h2 * 1024:(h2 + 1) * 1024],
                          pe_waits=(("sIN", xo_in),))
    scr_done = b.cnt["sSC"]

    b.wait("gp", "sSC", scr_done)
    pc_jobs = []
    for e_ in range(NEXP):
        for k in range(8):
            pc_jobs.append((w1_d[e_, k, :, :], w1f[e_, k, :, :]))
            pc_jobs.append((w2_d[e_, k, :, :], w2f[e_, k, :, :]))
        for g4 in range(8):
            pc_jobs.append((
                w3_d[e_, 4 * g4:4 * g4 + 4, :, :].rearrange("a p n -> p a n"),
                w3f[e_, 4 * g4:4 * g4 + 4, :, :].rearrange("a p n -> p a n")))
    for src, dst in pc_jobs:
        b.emit("gp", lambda e, d=dst, ss=src: e.dma_start(
            out=d, in_=ss).then_inc(SEM["sGP"], 16))
        b.bump("sGP", 16)
    precast_done = b.cnt["sGP"]

    b.barrier()
    if stop_phase <= 1:
        return finish()

    def vs_ones(e):
        e.memset(Ve.rearrange("p (s c) -> p s c", c=65)[:, :, 64:65], 1.0)
        e.memset(Vo.rearrange("p (s c) -> p s c", c=65)[:, :, 64:65],
                 1.0).then_inc(SEM["sDVE"], 1)
    b.emit("dve", vs_ones)
    vones = b.bump("sDVE")

    b.wait("sp", "sSC", scr_done)

    kv_loaded = {}
    pair_visits_done = 0
    prev_norm_dve = 0
    prev_out_cp = 0

    pend = None

    def epi_stage1(pp, pnd):
        b.wait("pe", "sDVE", max(pp["rcp_v"], pnd))
        for half in range(2):
            def bc(e, o=pB[0:64, 1024 + half * 512:1024 + half * 512 + 512],
                   x=den16[:, half * 512:half * 512 + 512]):
                return e.matmul(o, onesb[64:65, 0:64], x, start=True,
                                stop=True, skip_group_check=True)
            b.emit("pe", bc)
        bc_v = b.mark("pe", "sPE")
        b.wait("dve", "sPE", bc_v)
        if pp["he"] == 0:
            def nrm(e, hp=pp["hp"], coff=pp["coff"]):
                e.tensor_mul(
                    attno[0:64, hp * NT + coff:hp * NT + coff + 1024],
                    outs[0:64, 0:1024],
                    pB[0:64, 1024:2048]).then_inc(SEM["sDVE"], 1)
            b.emit("dve", nrm)
            return b.bump("sDVE")
        def nrm(e):
            e.tensor_mul(tmpo[0:64, :], outs[0:64, 0:1024],
                         pB[0:64, 1024:2048]).then_inc(SEM["sDVE"], 1)
        b.emit("dve", nrm)
        return b.bump("sDVE")

    def epi_stage2(pp, nrm_v):
        b.wait("pe", "sDVE", nrm_v)
        for half in range(2):
            def sh(e, o=pB[64:128, 1024 + half * 512:1024 + half * 512 + 512],
                   x=tmpo[0:64, half * 512:half * 512 + 512]):
                return e.matmul(o, ids[0:64, 0:64], x, start=True,
                                stop=True, skip_group_check=True)
            b.emit("pe", sh)
        sh_v = b.mark("pe", "sPE")
        b.wait("dve", "sPE", sh_v)
        def cpo(e, hp=pp["hp"], coff=pp["coff"]):
            e.tensor_copy(
                attno[64:128, hp * NT + coff:hp * NT + coff + 1024],
                pB[64:128, 1024:2048]).then_inc(SEM["sDVE"], 1)
        b.emit("dve", cpo)
        return b.bump("sDVE")

    for hp in range(8):
        vb = VTb[hp % 2]
        if hp == 0:
            b.emit("sp", lambda e, d=vb: e.dma_start(
                out=d, in_=vt_scr[:, 0, :]).then_inc(SEM["sSC"], 16))
            kv_loaded[("v", 0)] = b.bump("sSC", 16)
        vtb_ld = kv_loaded[("v", hp)]
        if hp > 0:
            b.wait("sp", "sPE", pair_visits_done)
        for dst, src in ((KTb, kt_scr[:, hp, :]), (KTd, ktd_scr[:, hp, :]),
                         (VTd, vtd_scr[:, hp, :])):
            b.emit("sp", lambda e, d=dst, s=src: e.dma_start(
                out=d, in_=s).then_inc(SEM["sSC"], 16))
            b.bump("sSC", 16)
        k_ld = b.cnt["sSC"]

        grp_evac = {}
        for g in range((NSLOT + 7) // 8):
            lo = g * 8
            hi = min(lo + 8, NSLOT)
            bank = g % 4
            b.wait("pe", "sSC", max(vtb_ld, k_ld))
            if g >= 4:
                b.wait("pe", "sDVE", grp_evac[g - 4])
            for u in range(lo, hi):
                if u < NBB:
                    src = vb[:, u * 128:(u + 1) * 128]
                else:
                    du = u - NBB
                    src = VTd[:, du * 128:(du + 1) * 128]
                def tp(e, o=pB16[:, bank * 1024 + (u - lo) * 128:
                                 bank * 1024 + (u - lo + 1) * 128], s=src):
                    return e.transpose(o, s, ids[:])
                b.emit("pe", tp)
            vtp = b.mark("pe", "sPE")
            b.wait("dve", "sPE", vtp)
            b.wait("dve", "sDVE", vones)
            n = hi - lo
            def ev(e, n=n, lo=lo, bank=bank):
                sr = pB16[:, bank * 1024:bank * 1024 + n * 128].rearrange(
                    "p (s c) -> p s c", c=128)
                ve = Ve[:, lo * 65:(lo + n) * 65].rearrange(
                    "p (s c) -> p s c", c=65)
                vo = Vo[:, lo * 65:(lo + n) * 65].rearrange(
                    "p (s c) -> p s c", c=65)
                e.tensor_copy(ve[:, :, 0:64], sr[:, :, 0:64])
                e.tensor_copy(vo[:, :, 0:64], sr[:, :, 64:128]).then_inc(
                    SEM["sDVE"], 1)
            b.emit("dve", ev)
            grp_evac[g] = b.bump("sDVE")
        vslots_done = b.cnt["sDVE"]

        if hp < 7:
            nxt = VTb[(hp + 1) % 2]
            b.wait("sp", "sPE", b.cnt["sPE"])
            b.emit("sp", lambda e, d=nxt, h=hp + 1: e.dma_start(
                out=d, in_=vt_scr[:, h, :]).then_inc(SEM["sSC"], 16))
            kv_loaded[("v", hp + 1)] = b.bump("sSC", 16)

        for he, base, vs in ((0, 0, Ve), (1, 64, Vo)):
            for ci, (coff, nbulk, mask_base, dslot) in enumerate(
                    ((0, NBA, 0, NBB), (1024, NBB, NBA, NBB + NDIAG))):
                exp_v = {}
                av_v = {}
                qk_m = {}
                mask_v = {}
                nvis = nbulk + NDIAG
                first_av = True
                seg_pe = b.cnt["sPE"]
                seg_act = b.cnt["sACT"]

                def visit_info(v):
                    diag = v >= nbulk
                    dd = v - nbulk
                    if diag:
                        return (True, dd, dslot + dd, KTd, coff + dd * 128)
                    return (False, dd, v, KTb, v * 128)

                def emit_qk(v):
                    diag, dd, slot_v, kt_ap, kt_off = visit_info(v)
                    par = v % 2
                    ps = pA[:, par * 1024:(par + 1) * 1024]
                    b.wait("pe", "sACT", seg_act + v - 1 if v >= 2 else seg_act)
                    for half in range(2):
                        def qk(e, o=ps[:, half * 512:half * 512 + 512],
                               w=kt_ap[base:base + 64, kt_off:kt_off + 128],
                               x=QT[base:base + 64,
                                    hp * NT + coff + half * 512:
                                    hp * NT + coff + half * 512 + 512]):
                            return e.matmul(o, w, x, start=True, stop=True,
                                            skip_group_check=True)
                        b.emit("pe", qk)
                    qk_m[v] = b.mark("pe", "sPE")
                    if diag:
                        b.wait("dve", "sPE", qk_m[v])
                        def msk(e, dd=dd, ps=ps):
                            i = None
                            if dd > 0:
                                e.tensor_scalar_add(ps[:, 0:dd * 128],
                                                    ps[:, 0:dd * 128], NEG)
                            i = e.tensor_add(
                                ps[:, dd * 128:(dd + 1) * 128],
                                ps[:, dd * 128:(dd + 1) * 128], tri[:])
                            i.then_inc(SEM["sDVE"], 1)
                        b.emit("dve", msk)
                        mask_v[v] = b.bump("sDVE")

                emit_qk(0)
                for v in range(nvis):
                    diag, dd, slot_v, kt_ap, kt_off = visit_info(v)
                    par = v % 2
                    ps = pA[:, par * 1024:(par + 1) * 1024]
                    if v + 1 < nvis:
                        emit_qk(v + 1)
                    atb = at[v % 2]
                    if v in mask_v:
                        b.wait("act", "sDVE", mask_v[v])
                    else:
                        b.wait("act", "sPE", qk_m[v])
                    b.wait("act", "sPE", av_v[v - 2] if v >= 2 else seg_pe)
                    bias_arg = 0.0 if diag else \
                        amask[:, mask_base + v:mask_base + v + 1]
                    def ex(e, o=atb[:, 0:1024], ss=ps, bb=bias_arg):
                        e.activation(o, ss, AF.Exp, bias=bb,
                                     scale=1.0).then_inc(SEM["sACT"], 1)
                    b.emit("act", ex)
                    exp_v[v] = b.bump("sACT")
                    assert exp_v[v] == seg_act + v + 1
                    if first_av:
                        b.wait("pe", "sDVE", max(vslots_done, prev_out_cp))
                        first_av = False
                    b.wait("pe", "sACT", exp_v[v])
                    for half in range(2):
                        def av(e, o=pB[0:65, half * 512:half * 512 + 512],
                               w=vs[:, slot_v * 65:(slot_v + 1) * 65],
                               x=atb[:, half * 512:half * 512 + 512],
                               st=(v == 0), so=(v == nvis - 1)):
                            return e.matmul(o, w, x, start=st, stop=so,
                                            skip_group_check=True)
                        b.emit("pe", av)
                    def warm(e, o=pA[:, par * 1024:par * 1024 + 512]):
                        return e.matmul(o, ids[:], QT[0:128, 0:512],
                                        start=True, stop=True,
                                        skip_group_check=True)
                    b.emit("pe", warm)
                    av_v[v] = b.mark("pe", "sPE")
                    if pend is not None:
                        if pend["stage"] == 0 and v >= 1:
                            nv = epi_stage1(pend, prev_norm_dve)
                            if pend["he"] == 0:
                                prev_norm_dve = nv
                                pend = None
                            else:
                                pend["stage"] = 1
                                pend["nrm_v"] = nv
                        elif pend["stage"] == 1 and v >= 3:
                            prev_norm_dve = epi_stage2(pend, pend["nrm_v"])
                            pend = None

                avl = av_v[nvis - 1]
                b.wait("dve", "sPE", avl)
                def cp(e):
                    with nc.allow_low_precision(reason="f16 1/den operand"):
                        e.reciprocal(den16, pB[64:65, 0:1024])
                    e.tensor_copy(outs[0:65, 0:1024],
                                  pB[0:65, 0:1024]).then_inc(SEM["sDVE"], 1)
                b.emit("dve", cp)
                prev_out_cp = b.bump("sDVE")
                pend = dict(stage=0, rcp_v=prev_out_cp, he=he, hp=hp,
                            coff=coff)
        if pend is not None:
            nv = epi_stage1(pend, prev_norm_dve)
            if pend["he"] == 0:
                prev_norm_dve = nv
            else:
                prev_norm_dve = epi_stage2(pend, nv)
            pend = None
        b.wait("pe", "sDVE", prev_norm_dve)
        pair_visits_done = b.cnt["sPE"]

    attno_done = b.cnt["sDVE"]
    b.barrier()
    if stop_phase <= 2:
        return finish()
    for _i in range(4):
        psl_free[_i] = 0

    b.emit("sp", lambda e: e.dma_start(out=wob, in_=wo_d).then_inc(
        SEM["sIN"], 16))
    b.bump("sIN", 16)
    b.emit("sp", lambda e: e.dma_start(
        out=xown2.rearrange("p (k t) -> p k t", k=8),
        in_=xownT_d).then_inc(SEM["sIN"], 16))
    wo_ld = b.bump("sIN", 16)
    b.wait("gp", "sBAR", b.cnt["sBAR"])
    b.emit("gp", lambda e: e.dma_start(
        out=wrs, in_=wr32_d.rearrange("p k e -> p (k e)")).then_inc(
        SEM["sGP"], 16))
    wrs_ld = b.bump("sGP", 16)

    slot3 = 0
    for m in range(8):
        for h2 in range(2):
            sl = slot3 % 4; slot3 += 1
            ps = PSL[sl]
            b.wait("pe", "sIN", wo_ld)
            if psl_free[sl]:
                b.wait("pe", "sDVE", psl_free[sl])
            for half in range(2):
                for k in range(8):
                    def mm(e, o=ps[:, half * 512:(half + 1) * 512],
                           w=wob[:, k * D + m * 128:k * D + (m + 1) * 128],
                           x=attno[:, k * NT + h2 * 1024 + half * 512:
                                   k * NT + h2 * 1024 + half * 512 + 512],
                           st=(k == 0), so=(k == 7)):
                        return e.matmul(o, w, x, start=st, stop=so,
                                 skip_group_check=True)
                    b.emit("pe", mm)
            vmm = b.mark("pe", "sPE")
            b.wait("dve", "sPE", vmm)
            def x1f(e, s=ps, m=m, h2=h2):
                e.scalar_tensor_tensor(
                    x2[:, m * NT + h2 * 1024:m * NT + h2 * 1024 + 1024],
                    s, boc[:, m:m + 1],
                    xown2[:, m * NT + h2 * 1024:m * NT + h2 * 1024 + 1024],
                    OP.add, OP.add).then_inc(SEM["sDVE"], 1)
            b.emit("dve", x1f)
            psl_free[sl] = b.bump("sDVE")
    x1_done = b.cnt["sDVE"]

    for hf in range(2):
        hoff = hf * 1024
        b.wait("pe", "sDVE", x1_done)
        sq_mm = {}
        for m in range(8):
            sqb = sq[m % 2]
            b.wait("act", "sDVE", x1_done)
            if m >= 2:
                b.wait("act", "sPE", sq_mm[m - 2])
            def sqf(e, m=m, dd=sqb, hoff=hoff):
                e.activation(dd[:, 0:1024], x2[:, m * NT + hoff:
                                               m * NT + hoff + 1024],
                             AF.Square, bias=0.0, scale=1.0).then_inc(
                    SEM["sACT"], 1)
            b.emit("act", sqf)
            sq_done = b.bump("sACT")
            for half in range(2):
                def mmm(e, o=pA[0:1, half * 512:(half + 1) * 512],
                        x=x2[:, m * NT + hoff + half * 512:
                             m * NT + hoff + half * 512 + 512],
                        st=(m == 0), so=(m == 7)):
                    return e.matmul(o, ones_col[:], x, start=st, stop=so,
                             skip_group_check=True)
                b.emit("pe", mmm)
            b.wait("pe", "sACT", sq_done)
            for half in range(2):
                def mms(e, o=pA[0:1, 1024 + half * 512:1024 + (half + 1) * 512],
                        x=sqb[:, half * 512:half * 512 + 512],
                        st=(m == 0), so=(m == 7)):
                    return e.matmul(o, ones_col[:], x, start=st, stop=so,
                             skip_group_check=True)
                b.emit("pe", mms)
            sq_mm[m] = b.mark("pe", "sPE")
        stats_done = sq_mm[7]
        b.wait("dve", "sPE", stats_done)
        rg = b.dvechain([
            lambda e: e.tensor_scalar_mul(rows[:, 0:1024], pA[0:1, 0:1024],
                                          1.0 / D),
            lambda e: e.tensor_scalar_mul(rows[:, 1024:2048],
                                          pA[0:1, 1024:2048], 1.0 / D),
            lambda e: e.tensor_mul(rows[:, 2048:3072], rows[:, 0:1024],
                                   rows[:, 0:1024]),
            lambda e: e.tensor_sub(rows[:, 1024:2048], rows[:, 1024:2048],
                                   rows[:, 2048:3072]),
            lambda e: e.tensor_scalar_add(rows[:, 1024:2048],
                                          rows[:, 1024:2048], EPS)])
        b.wait("act", "sDVE", rg)
        def rsd(e):
            e.activation(rows[:, 2048:3072], rows[:, 1024:2048], AF.Sqrt,
                         bias=0.0, scale=1.0).then_inc(SEM["sACT"], 1)
        b.emit("act", rsd)
        rsv = b.bump("sACT")
        b.wait("dve", "sACT", rsv)
        r16 = b.dvechain([
            lambda e: e.reciprocal(rows[:, 2048:3072], rows[:, 2048:3072]),
            lambda e: e.tensor_copy(rows16[:, 0:1024], rows[:, 0:1024]),
            lambda e: e.tensor_copy(rows16[:, 1024:2048],
                                    rows[:, 2048:3072])])
        b.wait("pe", "sDVE", r16)
        for half in range(2):
            def bm(e, h=half):
                return e.matmul(pA[:, h * 512:(h + 1) * 512], ones_row[0:1, 0:128],
                         rows16[:, h * 512:h * 512 + 512], start=True,
                         stop=True, skip_group_check=True)
            b.emit("pe", bm)
            def br_(e, h=half):
                return e.matmul(pB[:, h * 512:(h + 1) * 512], ones_row[0:1, 0:128],
                         rows16[:, 1024 + h * 512:1024 + h * 512 + 512],
                         start=True, stop=True, skip_group_check=True)
            b.emit("pe", br_)
        bcast_v = b.mark("pe", "sPE")
        b.wait("dve", "sPE", bcast_v)
        for m in range(8):
            sl_ = x2[:, m * NT + hoff:m * NT + hoff + 1024]
            x1_done = b.dvechain([
                lambda e, sl_=sl_: e.tensor_sub(sl_, sl_, pA[0:128, 0:1024]),
                lambda e, sl_=sl_: e.tensor_mul(sl_, sl_, pB[0:128, 0:1024])])
    x2_done = x1_done
    if debug:
        b.wait("sp", "sDVE", x2_done)
        b.emit("sp", lambda e: e.dma_start(out=dbg_x2, in_=A[2][:, :]).then_inc(
            SEM["sOUT"], 16))
        b.bump("sOUT", 16)
        b.wait("sp", "sDVE", attno_done)
        b.emit("sp", lambda e: e.dma_start(out=dbg_at, in_=A[1][:, :]).then_inc(
            SEM["sOUT"], 16))
        b.bump("sOUT", 16)

    b.wait("pe", "sDVE", x2_done)
    b.wait("pe", "sGP", wrs_ld)
    for half in range(4):
        for k in range(8):
            def mm(e, h=half, k=k):
                return e.matmul(pA[0:8, h * 512:(h + 1) * 512],
                         wrs[:, k * 8:(k + 1) * 8],
                         x2[:, k * NT + h * 512:k * NT + h * 512 + 512],
                         start=(k == 0), stop=False, skip_group_check=True)
            b.emit("pe", mm)
        def mmb(e, h=half):
            return e.matmul(pA[0:8, h * 512:(h + 1) * 512], brr[:],
                     ones_row[0:1, 0:512], start=False, stop=True,
                     skip_group_check=True)
        b.emit("pe", mmb)
    log_mm = b.mark("pe", "sPE")
    b.wait("act", "sPE", log_mm)
    def logev(e):
        e.activation(logT, pA[0:8, 0:2048], AF.Copy, bias=0.0,
                     scale=1.0).then_inc(SEM["sACT"], 1)
    b.emit("act", logev)
    log_v = b.bump("sACT")
    b.wait("pe", "sACT", log_v)
    for t in range(16):
        def tp(e, t=t):
            return e.transpose(pB[:, t * 8:(t + 1) * 8],
                        logT[:, t * 128:(t + 1) * 128], ids32[0:8, 0:8])
        b.emit("pe", tp)
    ltp_v = b.mark("pe", "sPE")
    b.wait("dve", "sPE", ltp_v)
    def lcp(e):
        e.tensor_copy(ltm, pB[:, 0:128]).then_inc(SEM["sDVE"], 1)
    b.emit("dve", lcp)
    ltm_v = b.bump("sDVE")
    b.wait("dve", "sDVE", ltm_v)

    for t in range(16):
        L = ltm[:, t * 8:(t + 1) * 8]
        C = ctm[:, t * 8:(t + 1) * 8]
        m1 = t2a[:, 0:1]; m2 = t2a[:, 1:2]
        s1c = t2a[:, 2:3]; s2c = t2a[:, 3:4]
        dm = t2a[:, 4:5]; w1_ = t2a[:, 5:6]; w2_ = t2a[:, 6:7]
        oh1 = t2b; oh2 = t2c
        dv = b.dvechain([
            lambda e, L=L, m1=m1: e.tensor_reduce(
                m1, L, mybir.AxisListType.X, OP.max),
            lambda e, L=L, m1=m1, oh1=oh1: e.tensor_single_scalar(
                oh1, L, m1, OP.is_ge),
            lambda e, oh1=oh1, oh2=oh2: e.tensor_mul(oh2, oh1, rank8[:]),
            lambda e, oh2=oh2, s1c=s1c: e.tensor_reduce(
                s1c, oh2, mybir.AxisListType.X, OP.max),
            lambda e, oh1=oh1, oh2=oh2, s1c=s1c: e.tensor_single_scalar(
                oh1, oh2, s1c, OP.is_equal),
            lambda e, C=C, oh1=oh1, L=L: e.scalar_tensor_tensor(
                C, oh1, -1e30, L, OP.mult, OP.add),
            lambda e, C=C, m2=m2: e.tensor_reduce(
                m2, C, mybir.AxisListType.X, OP.max),
            lambda e, C=C, m2=m2, oh2=oh2: e.tensor_single_scalar(
                oh2, C, m2, OP.is_ge),
            lambda e, oh2=oh2: e.tensor_mul(oh2, oh2, rank8[:]),
            lambda e, oh2=oh2, s2c=s2c: e.tensor_reduce(
                s2c, oh2, mybir.AxisListType.X, OP.max),
            lambda e, oh2=oh2, s2c=s2c: e.tensor_single_scalar(
                oh2, oh2, s2c, OP.is_equal),
            lambda e, m1=m1, m2=m2, dm=dm: e.tensor_sub(dm, m1, m2)])
        b.wait("act", "sDVE", dv)
        def a1(e, dm=dm, w1_=w1_):
            e.activation(w1_, dm, AF.Sigmoid, bias=0.0,
                         scale=1.0).then_inc(SEM["sACT"], 1)
        b.emit("act", a1)
        av_ = b.bump("sACT")
        b.wait("dve", "sACT", av_)
        b.dvechain([
            lambda e, w1_=w1_, w2_=w2_: e.tensor_scalar(
                w2_, w1_, -1.0, 1.0, OP.mult, OP.add),
            lambda e, C=C, oh1=oh1, w1_=w1_: e.tensor_scalar_mul(C, oh1, w1_),
            lambda e, C=C, oh2=oh2, w2_=w2_: e.scalar_tensor_tensor(
                C, oh2, w2_, C, OP.mult, OP.add)])
    cw_tm_v = b.cnt["sDVE"]
    b.wait("pe", "sDVE", cw_tm_v)
    for t in range(16):
        def tp(e, t=t):
            return e.transpose(pA[0:8, t * 128:(t + 1) * 128],
                        ctm[:, t * 8:(t + 1) * 8], ids32[:])
        b.emit("pe", tp)
    cwtp_v = b.mark("pe", "sPE")
    b.wait("dve", "sPE", cwtp_v)
    def cwc(e):
        e.tensor_copy(cwT, pA[0:8, 0:2048]).then_inc(SEM["sDVE"], 1)
    b.emit("dve", cwc)
    cwT_v = b.bump("sDVE")
    if debug:
        b.wait("sp", "sDVE", cwT_v)
        b.emit("sp", lambda e: e.dma_start(out=dbg_cw, in_=cwT).then_inc(
            SEM["sOUT"], 16))
        b.bump("sOUT", 16)
        b.emit("sp", lambda e: e.dma_start(out=dbg_lt, in_=logT).then_inc(
            SEM["sOUT"], 16))
        b.bump("sOUT", 16)
        b.emit("sp", lambda e: e.dma_start(out=dbg_tm[:, 0:128],
                                           in_=ltm).then_inc(SEM["sOUT"], 16))
        b.bump("sOUT", 16)
        b.emit("sp", lambda e: e.dma_start(out=dbg_tm[:, 128:256],
                                           in_=ctm).then_inc(SEM["sOUT"], 16))
        b.bump("sOUT", 16)

    b.barrier()
    if stop_phase <= 3:
        return finish()

    b.wait("sp", "sGP", precast_done)

    w12_loaded = {}
    w12_freed = {}
    w3_loaded = {}
    w3_freed = {}

    def emit_w12_dma(p, e_, mg):
        par = mg % 2
        if (par,) in w12_freed:
            b.wait("sp", "sPE", w12_freed[(par,)])
        for wd, wbuf in ((w1f, w1b[par]), (w2f, w2b[par])):
            for k in range(8):
                def f(e, d=wbuf[:, k * 512:(k + 1) * 512],
                      s=wd[e_, k, :, mg * 512:(mg + 1) * 512]):
                    e.dma_start(out=d, in_=s).then_inc(SEM["sW"], 16)
                b.emit("sp", f)
                b.bump("sW", 16)
        w12_loaded[(p, e_, mg)] = b.cnt["sW"]

    def emit_w3_dma(p, e_, blk, kk):
        par = kk % 2
        if (par,) in w3_freed:
            b.wait("sp", "sPE", w3_freed[(par,)])
        def f(e, d=w3b[par][:, 0:512].rearrange("p (a n) -> p a n", a=2),
              s=w3f[e_, 2 * kk:2 * kk + 2, :, blk * 256:(blk + 1) * 256]
              .rearrange("a p n -> p a n")):
            e.dma_start(out=d, in_=s).then_inc(SEM["sW"], 16)
        b.emit("sp", f)
        b.bump("sW", 16)
        w3_loaded[(p, e_, blk, kk)] = b.cnt["sW"]

    def ybank(tc, ml):
        arena = pA if tc == 0 else pB
        return arena[:, ml * 512:(ml + 1) * 512]

    cwbs = [cwb, cwb2]
    free_ar = {0: 0, 1: 0}

    def emit_cwb(pp, ee):
        tbb = pp * 1024
        for half in range(2):
            def bcw(e, h=half, ee=ee, tbb=tbb):
                return e.matmul(pA[:, h * 512:(h + 1) * 512],
                                e8sel[:, ee * 128:(ee + 1) * 128],
                                cwT[:, tbb + h * 512:tbb + h * 512 + 512],
                                start=True, stop=True, skip_group_check=True)
            b.emit("pe", bcw)
        cmm = b.mark("pe", "sPE")
        b.wait("dve", "sPE", cmm)
        def cwbe(e, d=cwbs[(pp * 8 + ee) % 2]):
            e.tensor_copy(d, pA[:, 0:1024]).then_inc(SEM["sDVE"], 1)
        b.emit("dve", cwbe)
        return b.bump("sDVE")

    for p in range(2):
        tb = p * 1024

        for grp in range(2):
            b.wait("pe", "sDVE", cwT_v)
            for ml in range(4):
                m = grp * 4 + ml
                for tc in range(2):
                    b.wait("pe", "sDVE", free_ar[tc])
                    def mm(e, o=ybank(tc, ml), m=m, tc=tc, tb=tb):
                        return e.matmul(o, b3s[:, m * 128:(m + 1) * 128],
                                 cwT[:, tb + tc * 512:tb + tc * 512 + 512],
                                 start=True, stop=True, skip_group_check=True)
                    b.emit("pe", mm)
            vinit_mm = b.mark("pe", "sPE")
            b.wait("dve", "sPE", vinit_mm)
            def vev(e, grp=grp):
                i = None
                for ml in range(4):
                    m = grp * 4 + ml
                    for tc in range(2):
                        i = e.tensor_copy(
                            vacc[:, m * 1024 + tc * 512:
                                 m * 1024 + tc * 512 + 512],
                            ybank(tc, ml))
                i.then_inc(SEM["sDVE"], 1)
            b.emit("dve", vev)
            vv = b.bump("sDVE")
            free_ar[0] = vv
            free_ar[1] = vv

        if p == 0:
            b.wait("pe", "sDVE", free_ar[0])
            b.wait("pe", "sDVE", cwT_v)
            cwb_v = emit_cwb(0, 0)
            free_ar[0] = cwb_v

        for e_ in range(NEXP):
            mycwb = cwbs[(p * 8 + e_) % 2]
            if (p, e_, 0) not in w12_loaded:
                emit_w12_dma(p, e_, 0)
            if (p, e_, 1) not in w12_loaded:
                emit_w12_dma(p, e_, 1)

            if e_ < NEXP - 1:
                nxt_pe = (p, e_ + 1)
            elif p == 0:
                nxt_pe = (1, 0)
            else:
                nxt_pe = None

            silu_v = {}
            mult_v = {}
            for m in range(32):
                mg = m // 4
                par = mg % 2
                ml = m % 4
                if ml == 0 and mg >= 1:
                    tgt = mg + 1
                    if tgt <= 7:
                        emit_w12_dma(p, e_, tgt)
                    elif nxt_pe is not None:
                        emit_w12_dma(nxt_pe[0], nxt_pe[1], 0)
                arena = pA if m % 2 == 0 else pB
                h1 = arena[:, 0:1024]
                h2 = arena[:, 1024:2048]
                if m >= 2:
                    b.wait("pe", "sACT", silu_v[m - 2])
                    b.wait("pe", "sDVE", mult_v[m - 2])
                elif m == 0:
                    b.wait("pe", "sDVE", cwb_v)
                    b.wait("pe", "sDVE", free_ar[0])
                else:
                    b.wait("pe", "sDVE", free_ar[1])
                b.wait("pe", "sW", w12_loaded[(p, e_, mg)])
                for k in range(8):
                    for half in range(2):
                        def mm(e, o=h1[:, half * 512:half * 512 + 512],
                               w=w1b[par][:, k * 512 + ml * 128:
                                          k * 512 + ml * 128 + 128],
                               x=x2[:, k * NT + tb + half * 512:
                                    k * NT + tb + half * 512 + 512],
                               st=(k == 0), so=(k == 7)):
                            return e.matmul(o, w, x, start=st, stop=so,
                                     skip_group_check=True)
                        b.emit("pe", mm)
                h1_mm = b.mark("pe", "sPE")
                for k in range(8):
                    for half in range(2):
                        def mm(e, o=h2[:, half * 512:half * 512 + 512],
                               w=w2b[par][:, k * 512 + ml * 128:
                                          k * 512 + ml * 128 + 128],
                               x=x2[:, k * NT + tb + half * 512:
                                    k * NT + tb + half * 512 + 512],
                               st=(k == 0), so=(k == 7)):
                            return e.matmul(o, w, x, start=st, stop=so,
                                     skip_group_check=True)
                        b.emit("pe", mm)
                h2_mm = b.mark("pe", "sPE")
                if ml == 3:
                    w12_freed[(par,)] = h2_mm
                b.wait("act", "sPE", h1_mm)
                if m >= 2:
                    b.wait("act", "sDVE", mult_v[m - 2])
                def sil(e, s=h1, dd=s1b[m % 2], e_=e_, m=m):
                    e.activation(dd, s, AF.Silu,
                                 bias=b1s[:, e_ * 32 + m:e_ * 32 + m + 1],
                                 scale=1.0).then_inc(SEM["sACT"], 1)
                b.emit("act", sil)
                silu_v[m] = b.bump("sACT")
                b.wait("dve", "sPE", h2_mm)
                b.wait("dve", "sACT", silu_v[m])
                def mlt(e, s=h2, dd=hb(m), sc=s1b[m % 2], e_=e_, m=m):
                    e.scalar_tensor_tensor(
                        dd, s, b2s[:, e_ * 32 + m:e_ * 32 + m + 1], sc,
                        OP.add, OP.mult).then_inc(SEM["sDVE"], 1)
                b.emit("dve", mlt)
                mult_v[m] = b.bump("sDVE")

            if nxt_pe is not None and (nxt_pe[0], nxt_pe[1], 1) not in w12_loaded:
                emit_w12_dma(nxt_pe[0], nxt_pe[1], 1)
            yev_blk = {}
            for blk in range(4):
                ar = pA if blk % 2 == 0 else pB
                emit_w3_dma(p, e_, blk, 0)
                for kk in range(16):
                    if kk + 1 < 16:
                        emit_w3_dma(p, e_, blk, kk + 1)
                    b.wait("pe", "sW", w3_loaded[(p, e_, blk, kk)])
                    par = kk % 2
                    if kk == 0:
                        if blk == 0:
                            b.wait("pe", "sACT", silu_v[30])
                            b.wait("pe", "sDVE", mult_v[30])
                            b.wait("pe", "sDVE", free_ar[0])
                        elif blk == 1:
                            b.wait("pe", "sACT", silu_v[31])
                            b.wait("pe", "sDVE", mult_v[31])
                            b.wait("pe", "sDVE", free_ar[1])
                        else:
                            b.wait("pe", "sDVE", yev_blk[blk - 2])
                            if blk == 2:
                                b.wait("pe", "sDVE", cwb_v)
                    if blk == 0 and kk == 15:
                        b.wait("pe", "sDVE", mult_v[31])
                    for ksub in range(2):
                        k = kk * 2 + ksub
                        for tc in range(2):
                            for ml2 in range(2):
                                def mm(e, o=ar[:, (ml2 * 2 + tc) * 512:
                                               (ml2 * 2 + tc + 1) * 512],
                                       w=w3b[par][:, ksub * 256 + ml2 * 128:
                                                  ksub * 256 + ml2 * 128 + 128],
                                       x=hb(k)[:, tc * 512:tc * 512 + 512],
                                       st=(k == 0), so=(k == 31)):
                                    return e.matmul(o, w, x, start=st, stop=so,
                                             skip_group_check=True)
                                b.emit("pe", mm)
                    kmark = b.mark("pe", "sPE")
                    w3_freed[(par,)] = kmark
                y_mm = b.cnt["sPE"]
                ncwb_v = None
                if blk == 1 and nxt_pe is not None:
                    b.wait("pe", "sDVE", yev_blk[0])
                    ncwb_v = emit_cwb(nxt_pe[0], nxt_pe[1])
                b.wait("dve", "sPE", y_mm)
                chain_v = None
                for ml2 in range(2):
                    for tc in range(2):
                        m = blk * 2 + ml2
                        vsl = vacc[:, m * 1024 + tc * 512:
                                   m * 1024 + tc * 512 + 512]
                        chain_v = b.dvechain([
                            lambda e, o=ar[:, (ml2 * 2 + tc) * 512:
                                           (ml2 * 2 + tc + 1) * 512],
                                   tc=tc, cb=mycwb: e.tensor_mul(
                                yst, o, cb[:, tc * 512:tc * 512 + 512]),
                            lambda e, vsl=vsl: e.tensor_add(vsl, vsl, yst)])
                yev_blk[blk] = chain_v
                free_ar[blk % 2] = chain_v
                if ncwb_v is not None:
                    cwb_v = ncwb_v

        for m in range(8):
            def resf(e, m=m, last=(m == 7), tb=tb):
                i = e.tensor_add(vacc[:, m * 1024:(m + 1) * 1024],
                                 vacc[:, m * 1024:(m + 1) * 1024],
                                 x2[:, m * NT + tb:m * NT + tb + 1024])
                if last:
                    i.then_inc(SEM["sDVE"], 1)
            b.emit("dve", resf)
        vres = b.bump("sDVE")
        b.wait("pe", "sDVE", vres)
        sqmm = {}
        for m in range(8):
            sqb = s1b[m % 2]
            b.wait("act", "sDVE", vres)
            if m >= 2:
                b.wait("act", "sPE", sqmm[m - 2])
            def sqf(e, m=m, dd=sqb):
                e.activation(dd, vacc[:, m * 1024:(m + 1) * 1024], AF.Square,
                             bias=0.0, scale=1.0).then_inc(SEM["sACT"], 1)
            b.emit("act", sqf)
            sqv = b.bump("sACT")
            for half in range(2):
                def mmm(e, o=pA[0:1, half * 512:(half + 1) * 512],
                        x=vacc[:, m * 1024 + half * 512:
                               m * 1024 + half * 512 + 512],
                        st=(m == 0), so=(m == 7)):
                    return e.matmul(o, ones_col[:], x, start=st, stop=so,
                             skip_group_check=True)
                b.emit("pe", mmm)
            b.wait("pe", "sACT", sqv)
            for half in range(2):
                def mms(e, o=pA[0:1, 1024 + half * 512:1024 + (half + 1) * 512],
                        x=sqb[:, half * 512:half * 512 + 512],
                        st=(m == 0), so=(m == 7)):
                    return e.matmul(o, ones_col[:], x, start=st, stop=so,
                             skip_group_check=True)
                b.emit("pe", mms)
            sqmm[m] = b.mark("pe", "sPE")
        st2 = sqmm[7]
        b.wait("dve", "sPE", st2)
        rg = b.dvechain([
            lambda e: e.tensor_scalar_mul(rows[:, 0:1024], pA[0:1, 0:1024],
                                          1.0 / D),
            lambda e: e.tensor_scalar_mul(rows[:, 1024:2048],
                                          pA[0:1, 1024:2048], 1.0 / D),
            lambda e: e.tensor_mul(rows[:, 2048:3072], rows[:, 0:1024],
                                   rows[:, 0:1024]),
            lambda e: e.tensor_sub(rows[:, 1024:2048], rows[:, 1024:2048],
                                   rows[:, 2048:3072]),
            lambda e: e.tensor_scalar_add(rows[:, 1024:2048],
                                          rows[:, 1024:2048], EPS)])
        b.wait("act", "sDVE", rg)
        def rsd(e):
            e.activation(rows[:, 2048:3072], rows[:, 1024:2048], AF.Sqrt,
                         bias=0.0, scale=1.0).then_inc(SEM["sACT"], 1)
        b.emit("act", rsd)
        rsv = b.bump("sACT")
        b.wait("dve", "sACT", rsv)
        r16 = b.dvechain([
            lambda e: e.reciprocal(rows[:, 2048:3072], rows[:, 2048:3072]),
            lambda e: e.tensor_copy(rows16[:, 0:1024], rows[:, 0:1024]),
            lambda e: e.tensor_copy(rows16[:, 1024:2048],
                                    rows[:, 2048:3072])])
        b.wait("pe", "sDVE", r16)
        for half in range(2):
            def bm(e, h=half):
                return e.matmul(pA[:, h * 512:(h + 1) * 512], ones_row[0:1, 0:128],
                         rows16[:, h * 512:h * 512 + 512], start=True,
                         stop=True, skip_group_check=True)
            b.emit("pe", bm)
            def br_(e, h=half):
                return e.matmul(pB[:, h * 512:(h + 1) * 512], ones_row[0:1, 0:128],
                         rows16[:, 1024 + h * 512:1024 + h * 512 + 512],
                         start=True, stop=True, skip_group_check=True)
            b.emit("pe", br_)
        bc2 = b.mark("pe", "sPE")
        out_dma = {}
        for m in range(8):
            ob = ost[m % 2]
            b.wait("dve", "sPE", bc2)
            if m >= 2:
                b.wait("dve", "sOUT", out_dma[m - 2])
            elif p > 0:
                b.wait("dve", "sOUT", prev_pass_out)
            fv = b.dvechain([
                lambda e, m=m, dd=ob: e.tensor_sub(
                    dd, vacc[:, m * 1024:(m + 1) * 1024], pA[0:128, 0:1024]),
                lambda e, dd=ob: e.tensor_mul(dd, dd, pB[0:128, 0:1024])])
            b.wait("sp", "sDVE", fv)
            b.emit("sp", lambda e, dd=out_d[:, m, tb:tb + 1024], s=ob:
                   e.dma_start(out=dd, in_=s).then_inc(SEM["sOUT"], 16))
            out_dma[m] = b.bump("sOUT", 16)
        lastfin = b.cnt["sDVE"]
        prev_pass_out = b.cnt["sOUT"]
        free_ar[0] = lastfin
        free_ar[1] = lastfin

    return finish()


def _finish(b, nc):
    SEM = b.sem
    b.wait("sp", "sOUT", b.cnt["sOUT"])
    b.wait("sp", "sSC", b.cnt["sSC"])
    b.wait("sp", "sW", b.cnt["sW"])
    b.wait("sp", "sIN", b.cnt["sIN"])
    b.wait("gp", "sGP", b.cnt["sGP"])
    b.barrier()

    with nc.Block() as block:
        @block.tensor
        def _(e):
            for f in b.q["pe"]:
                f(e)

        @block.scalar
        def _(e):
            for f in b.q["act"]:
                f(e)

        @block.vector
        def _(e):
            for f in b.q["dve"]:
                f(e)

        @block.sync
        def _(e):
            for f in b.q["sp"]:
                f(e)

        @block.gpsimd
        def _(e):
            for f in b.q["gp"]:
                f(e)

    b.stack.close()
    return nc


def _prep_inputs(full):
    hid = np.asarray(full["L0_attn_hidden_states"])
    wq = np.asarray(full["L0_attn_wq"]); bq = np.asarray(full["L0_attn_bq"])
    wk = np.asarray(full["L0_attn_wk"]); bk = np.asarray(full["L0_attn_bk"])
    wv = np.asarray(full["L0_attn_wv"]); bv = np.asarray(full["L0_attn_bv"])
    wo = np.asarray(full["L0_attn_wo"]); bo = np.asarray(full["L0_attn_bo"])
    Wr = np.asarray(full["L0_ffn_Wr"]); br = np.asarray(full["L0_ffn_br"])
    W1 = np.asarray(full["L0_ffn_W1"]); b1 = np.asarray(full["L0_ffn_b1"])
    W2 = np.asarray(full["L0_ffn_W2"]); b2 = np.asarray(full["L0_ffn_b2"])
    W3 = np.asarray(full["L0_ffn_W3"]); b3 = np.asarray(full["L0_ffn_b3"])

    def wT_sb(w):
        wt = np.ascontiguousarray(w.astype(np.float16).T)
        return np.ascontiguousarray(
            wt.reshape(8, 128, D).transpose(1, 0, 2).reshape(128, 8 * D))

    def bias_cols(x):
        return np.ascontiguousarray(x.astype(np.float16).reshape(8, 128).T)

    def fm(xT):
        return np.ascontiguousarray(xT.reshape(8, 128, -1).transpose(1, 0, 2))

    tri_np = np.where(np.arange(128)[None, :] < np.arange(128)[:, None],
                      np.float32(NEG), np.float32(0.0)).astype(np.float32)
    rank8 = np.broadcast_to(np.arange(8, 0, -1, dtype=np.float32),
                            (128, 8)).copy()
    shared = {
        "wqT": wT_sb(wq), "wkT": wT_sb(wk), "wvT": wT_sb(wv), "woT": wT_sb(wo),
        "bq": bq.astype(np.float16).reshape(1, D),
        "bk": bk.astype(np.float16).reshape(1, D),
        "bv": bv.astype(np.float16).reshape(1, D),
        "bo": bias_cols(bo),
        "brr": br.astype(np.float16).reshape(1, 8),
        "ids16": np.eye(128, dtype=np.float16),
        "ids32": np.eye(128, dtype=np.float32),
        "tri": tri_np, "rank8": rank8,
        "wr32": np.ascontiguousarray(
            Wr.astype(np.float32).reshape(8, 128, 8).transpose(1, 0, 2)),
        "b1c": np.ascontiguousarray(
            b1.astype(np.float32).reshape(8, 32, 128).transpose(2, 0, 1)
            .reshape(128, 256)),
        "b2c": np.ascontiguousarray(
            b2.astype(np.float32).reshape(8, 32, 128).transpose(2, 0, 1)
            .reshape(128, 256)),
        "b3": b3.astype(np.float32),
        "e8sel": np.ascontiguousarray(
            np.kron(np.eye(8, dtype=np.float16),
                    np.ones((1, 128), np.float16))),
        "w1": np.ascontiguousarray(W1.astype(np.float32).reshape(8, 8, 128, DFF)),
        "w2": np.ascontiguousarray(W2.astype(np.float32).reshape(8, 8, 128, DFF)),
        "w3": np.ascontiguousarray(W3.astype(np.float32).reshape(8, 32, 128, D)),
    }

    in_maps = []
    metas = []
    for c in range(8):
        bt = c // 4
        j = c % 4
        hT = np.ascontiguousarray(hid[bt].T)
        cA = hT[:, 1024 * j:1024 * (j + 1)]
        cB = hT[:, 1024 * (7 - j):1024 * (8 - j)]
        am = np.zeros((128, NBA + NBB), dtype=np.float32)
        for v in range(NBA):
            if v >= 8 * j:
                am[:, v] = NEG
        for v in range(NBB):
            if v >= 8 * (7 - j):
                am[:, NBA + v] = NEG
        m = dict(shared)
        m["xkvT"] = fm(hT)
        m["xownT"] = fm(np.ascontiguousarray(np.concatenate([cA, cB], axis=1)))
        m["amask"] = am
        in_maps.append(m)
        metas.append((bt, j))
    return in_maps, metas


_NC_CACHE = {}


def kernel(**inputs):
    in_maps, metas = _prep_inputs(inputs)
    if "nc" not in _NC_CACHE:
        _NC_CACHE["nc"] = build_program()
    nc = _NC_CACHE["nc"]
    res = run_bass_kernel_spmd(nc, in_maps, core_ids=list(range(8)))
    out = np.zeros((2, T, D), dtype=np.float32)
    for c, (bt, j) in enumerate(metas):
        y = res.results[c]["y"]
        yT = np.asarray(y).transpose(1, 0, 2).reshape(D, NT)
        out[bt, 1024 * j:1024 * (j + 1), :] = yT[:, 0:1024].T
        out[bt, 1024 * (7 - j):1024 * (8 - j), :] = yT[:, 1024:2048].T
    return out


# version 13
# speedup vs baseline: 1.0721x; 1.0721x over previous
import numpy as np
from contextlib import ExitStack

import concourse.bass as bass
import concourse.mybir as mybir
from concourse.bass_utils import run_bass_kernel_spmd

F32 = mybir.dt.float32
F16 = mybir.dt.float16
AF = mybir.ActivationFunctionType
OP = mybir.AluOpType

D = 1024
T = 8192
NT = 2048
NBA = 24
NBB = 56
NDIAG = 8
KVB = NBB * 128
DFF = 4096
NEXP = 8
EPS = 1e-5
NEG = -10000.0

ENGS = ("pe", "act", "dve", "sp", "gp")


class Bld:
    def __init__(self):
        self.nc = bass.Bass("TRN2", target_bir_lowering=False)
        self.q = {e: [] for e in ENGS}
        self.cnt = {}
        self.sem = {}
        self.stack = ExitStack()

    def mksem(self, name):
        self.sem[name] = self.stack.enter_context(self.nc.semaphore(name))
        self.cnt[name] = 0

    def emit(self, eng, fn):
        self.q[eng].append(fn)

    def wait(self, eng, name, val):
        if val > 0:
            self.q[eng].append(lambda e, s=self.sem[name], v=val: e.wait_ge(s, v))

    def bump(self, name, k=1):
        self.cnt[name] += k
        return self.cnt[name]

    def mark(self, eng, name):
        v = self.bump(name)
        f = self.q[eng].pop()
        def g(e, f=f, s=self.sem[name]):
            inst = f(e)
            assert inst is not None, "mark() needs the closure to return inst"
            inst.then_inc(s, 1)
        self.q[eng].append(g)
        return v

    BARRIER_ENGS = ("pe", "act", "dve", "sp")

    def barrier(self):
        tgt = self.cnt["sBAR"] + len(self.BARRIER_ENGS)
        for eng in self.BARRIER_ENGS:
            def f(e, s=self.sem["sBAR"], t=tgt, eng=eng):
                if eng == "dve":
                    e.memset(self.barsc[:], 0.0).then_inc(s, 1)
                elif eng == "act":
                    e.activation(self.barsc[:], self.barsc[:],
                                 AF.Square, bias=0.0,
                                 scale=1.0).then_inc(s, 1)
                else:
                    e.nop().then_inc(s, 1)
                e.wait_ge(s, t)
            self.q[eng].append(f)
        self.cnt["sBAR"] = tgt

    def sbuf(self, name, shape, dt):
        return self.stack.enter_context(self.nc.sbuf_tensor(name, shape, dt))

    def dvechain(self, ops, inc_last=True):
        vals = [self.bump("sDVE") for _ in range(len(ops) - 1)]
        last = self.bump("sDVE") if inc_last else None
        def g(e, ops=ops, vals=vals, s=self.sem["sDVE"]):
            inst = None
            for i, op in enumerate(ops):
                inst = op(e)
                if i < len(ops) - 1:
                    inst.then_inc(s, 1)
                    e.wait_ge(s, vals[i])
            if inc_last:
                inst.then_inc(s, 1)
        self.q["dve"].append(g)
        return self.cnt["sDVE"]


def build_program(debug=False, stop_phase=99, p1a_chunks=7,
                  p1a_mode='full', do_p1b=True):
    b = Bld()
    nc = b.nc

    def din(name, shape, dt):
        return nc.dram_tensor(name, shape, dt, kind="ExternalInput").ap()

    xkvT_d = din("xkvT", [128, 8, T], F16)
    xownT_d = din("xownT", [128, 8, NT], F16)
    wq_d = din("wqT", [128, 8 * D], F16)
    wk_d = din("wkT", [128, 8 * D], F16)
    wv_d = din("wvT", [128, 8 * D], F16)
    wo_d = din("woT", [128, 8 * D], F16)
    bq_d = din("bq", [1, D], F16)
    bk_d = din("bk", [1, D], F16)
    bv_d = din("bv", [1, D], F16)
    bo_d = din("bo", [128, 8], F16)
    brr_d = din("brr", [1, 8], F16)
    amask_d = din("amask", [128, NBA + NBB], F32)
    ids_d = din("ids16", [128, 128], F16)
    ids32_d = din("ids32", [128, 128], F32)
    tri_d = din("tri", [128, 128], F32)
    rank8_d = din("rank8", [128, 8], F32)
    wr32_d = din("wr32", [128, 8, 8], F32)
    b1c_d = din("b1c", [128, 256], F32)
    b2c_d = din("b2c", [128, 256], F32)
    b3_d = din("b3", [8, D], F32)
    e8_d = din("e8sel", [8, 1024], F16)
    w1_d = din("w1", [NEXP, 8, 128, DFF], F32)
    w2_d = din("w2", [NEXP, 8, 128, DFF], F32)
    w3_d = din("w3", [NEXP, 4, 16, 128, 512], F32)
    out_d = nc.dram_tensor("y", [128, 8, NT], F32, kind="ExternalOutput").ap()
    if debug:
        dbg_x2 = nc.dram_tensor("dbg_x2", [128, 8 * NT], F16,
                                kind="ExternalOutput").ap()
        dbg_at = nc.dram_tensor("dbg_at", [128, 8 * NT], F16,
                                kind="ExternalOutput").ap()
        dbg_cw = nc.dram_tensor("dbg_cw", [8, NT], F16,
                                kind="ExternalOutput").ap()
        dbg_lt = nc.dram_tensor("dbg_lt", [8, NT], F32,
                                kind="ExternalOutput").ap()
        dbg_tm = nc.dram_tensor("dbg_tm", [128, 256], F32,
                                kind="ExternalOutput").ap()

    w1f = nc.dram_tensor("w1f", [NEXP, 8, 128, DFF], F16, kind="Internal").ap()
    w2f = nc.dram_tensor("w2f", [NEXP, 8, 128, DFF], F16, kind="Internal").ap()
    w3f = nc.dram_tensor("w3f", [NEXP, 4, 16, 128, 512], F16,
                         kind="Internal").ap()
    kt_scr = nc.dram_tensor("kt_scr", [128, 8, KVB], F16, kind="Internal").ap()
    vt_scr = nc.dram_tensor("vt_scr", [128, 8, KVB], F16, kind="Internal").ap()
    ktd_scr = nc.dram_tensor("ktd_scr", [128, 8, NT], F16, kind="Internal").ap()
    vtd_scr = nc.dram_tensor("vtd_scr", [128, 8, NT], F16, kind="Internal").ap()

    for s in ("sIN", "sSC", "sW", "sOUT", "sPE", "sACT", "sDVE", "sBAR", "sGP"):
        b.mksem(s)
    SEM = b.sem

    ids = b.sbuf("ids", [128, 128], F16)
    ids32 = b.sbuf("ids32s", [128, 128], F32)
    tri = b.sbuf("tris", [128, 128], F32)
    amask = b.sbuf("amasks", [128, NBA + NBB], F32)
    rank8 = b.sbuf("rank8s", [128, 8], F32)
    bqc = b.sbuf("bqc", [1, D], F16)
    bkc = b.sbuf("bkc", [1, D], F16)
    bvc = b.sbuf("bvc", [1, D], F16)
    boc = b.sbuf("boc", [128, 8], F16)
    brr = b.sbuf("brrs", [1, 8], F16)
    b3s = b.sbuf("b3s", [8, D], F16)
    b1s = b.sbuf("b1s", [128, 256], F32)
    b2s = b.sbuf("b2s", [128, 256], F32)
    e8sel = b.sbuf("e8sels", [8, 1024], F16)
    ones_col = b.sbuf("ones_col", [128, 1], F16)
    b.barsc = b.sbuf("barsc", [128, 1], F16)
    ones_row = b.sbuf("ones_row", [1, 512], F16)
    onesb = b.sbuf("onesb", [128, 64], F16)

    A = [b.sbuf(f"arena{i}", [128, 16384], F16) for i in range(6)]
    A32 = [a[:].bitcast(F32) for a in A]

    pA = b.stack.enter_context(nc.psum_tensor("pA", [128, 2048], F32))
    pB = b.stack.enter_context(nc.psum_tensor("pB", [128, 2048], F32))
    pB16 = pB[:].bitcast(F16)

    xown = A[0][:, 0:16384]
    wqb = A[1][:, 0:8192]
    xcb = [A[3][:, 0:8192], A[3][:, 8192:16384]]
    wkb = A[4][:, 0:8192]
    wvb = A[4][:, 8192:16384]
    stgs = [A[5][:, 0:1024], A[5][:, 1024:2048]]

    attno = A[1][:, 0:16384]
    QT = A[2][:, 0:16384]
    KTb = A[3][:, 0:7168]
    VTb = [A[3][:, 7168:14336], A[4][:, 0:7168]]
    KTd = A[3][:, 14336:16384]
    VTd = A[4][:, 7168:9216]
    at = [A[4][:, 9216:11264], A[4][:, 11264:13312]]
    tmpo = A[4][:, 13312:14336]
    NSLOT = NBB + 2 * NDIAG
    Ve = A[5][:, 4096:4096 + NSLOT * 65]
    Vo = A[5][:, 8776:8776 + NSLOT * 65]
    outs = A32[5][:, 6728:7752]
    drow = A32[5][0:1, 0:1024]
    den16 = A[4][64:65, 14336:15360]

    xown2 = A[0][:, 0:16384]
    x2 = A[2][:, 0:16384]
    wob = A[4][:, 0:8192]
    sq = [A[3][:, 0:2048], A[3][:, 2048:4096]]
    logT = A32[3][0:8, 2048:4096]
    ltm = A32[3][:, 4096:4224]
    ctm = A32[3][:, 4224:4352]
    t2a = A32[3][:, 4352:4368]
    t2b = A32[3][:, 4368:4376]
    t2c = A32[3][:, 4376:4384]
    wrs = A[3][:, 8768:8832]
    rows = A32[5][0:1, 0:3072]
    rows16 = A[5][0:1, 6144:8192]
    cwT = A[5][0:8, 10240:12288]

    def hb(m):
        if m < 16:
            return A[0][:, m * 1024:(m + 1) * 1024]
        return A[1][:, (m - 16) * 1024:(m - 15) * 1024]

    w1b = [A[3][:, 0:4096], A[3][:, 4096:8192]]
    w2b = [A[3][:, 8192:12288], A[3][:, 12288:16384]]
    w3b = [A[4][:, 8192:9216], A[4][:, 9216:10240]]
    s1b = [A[4][:, 10240:11264], A[4][:, 11264:12288]]
    vacc = A[4][:, 0:8192]
    cwb = A[4][:, 12288:13312]
    cwb2 = A[4][:, 14336:15360]
    yst = A32[4][:, 6656:7168]
    ost = [A32[5][:, 6144:7168], A32[5][:, 7168:8192]]

    for dst, src in ((ids[:], ids_d), (ids32[:], ids32_d), (tri[:], tri_d),
                     (amask[:], amask_d), (rank8[:], rank8_d),
                     (bqc[:], bq_d), (bkc[:], bk_d), (bvc[:], bv_d),
                     (boc[:], bo_d), (brr[:], brr_d), (e8sel[:], e8_d),
                     (b1s[:], b1c_d), (b2s[:], b2c_d)):
        def f(e, d=dst, s=src):
            e.dma_start(out=d, in_=s).then_inc(SEM["sIN"], 16)
        b.emit("sp", f)
        b.bump("sIN", 16)
    consts_in = b.cnt["sIN"]
    b.emit("gp", lambda e: e.dma_start(out=b3s[:], in_=b3_d).then_inc(
        SEM["sGP"], 16))
    b.bump("sGP", 16)

    def dve_init(e, v=consts_in):
        e.wait_ge(SEM["sIN"], v)
        e.memset(ones_col[:], 1.0)
        e.memset(onesb[64:65, 0:64], 1.0)
        e.memset(ones_row[:], 1.0).then_inc(SEM["sDVE"], 1)
    b.emit("dve", dve_init)
    b.bump("sDVE", 1)

    def finish():
        return _finish(b, nc)

    if stop_phase <= 0:
        return finish()

    PSL = [pA[:, 0:1024], pA[:, 1024:2048], pB[:, 0:1024], pB[:, 1024:2048]]
    psl_free = [0, 0, 0, 0]
    stage_free = {}

    def proj_unit(u, wbuf, bias, scale, x_ap, dst_sb=None, stage=None,
                  scr_ap=None, pe_waits=()):
        m = u["m"]
        sl = u["slot"] % 4
        b.wait("pe", "sACT", psl_free[sl])
        for (snm, val) in pe_waits:
            b.wait("pe", snm, val)
        ps = PSL[sl]
        for half in range(2):
            for k in range(8):
                def mm(e, o=ps[:, half * 512:(half + 1) * 512],
                       w=wbuf[:, k * D + m * 128:k * D + (m + 1) * 128],
                       x=x_ap(k, half), st=(k == 0)):
                    return e.matmul(o, w, x, start=st, stop=False,
                             skip_group_check=True)
                b.emit("pe", mm)
            def mmb(e, o=ps[:, half * 512:(half + 1) * 512],
                    w=bias[0:1, m * 128:(m + 1) * 128]):
                return e.matmul(o, w, ones_row[0:1, 0:512], start=False, stop=True,
                         skip_group_check=True)
            b.emit("pe", mmb)
        vmm = b.mark("pe", "sPE")
        if p1a_mode == "mm_only":
            psl_free[sl] = 0
            return
        b.wait("act", "sPE", vmm)
        if dst_sb is not None:
            def ev(e, dd=dst_sb, s=ps, sc=scale):
                e.activation(dd, s, AF.Copy, bias=0.0, scale=sc).then_inc(
                    SEM["sACT"], 1)
            b.emit("act", ev)
            psl_free[sl] = b.bump("sACT")
        else:
            sid = id(stage)
            if sid in stage_free:
                b.wait("act", "sSC", stage_free[sid])
            def ev(e, dd=stage, s=ps, sc=scale):
                e.activation(dd, s, AF.Copy, bias=0.0, scale=sc).then_inc(
                    SEM["sACT"], 1)
            b.emit("act", ev)
            psl_free[sl] = b.bump("sACT")
            if p1a_mode == "no_dma":
                return
            b.wait("sp", "sACT", psl_free[sl])
            b.emit("sp", lambda e, s=stage, dd=scr_ap: e.dma_start(
                out=dd, in_=s).then_inc(SEM["sSC"], 16))
            stage_free[sid] = b.bump("sSC", 16)

    for dst, src in ((wkb, wk_d), (wvb, wv_d), (wqb, wq_d)):
        b.emit("sp", lambda e, d=dst, s=src: e.dma_start(
            out=d, in_=s).then_inc(SEM["sIN"], 16))
        b.bump("sIN", 16)
    wload = b.cnt["sIN"]

    b.wait("pe", "sDVE", 1)
    slot = 0
    chunk_pe = {}
    for c in range(p1a_chunks):
        buf = xcb[c % 2]
        if c >= 2:
            b.wait("sp", "sPE", chunk_pe[c - 2])
        b.emit("sp", lambda e, d=buf, cc=c: e.dma_start(
            out=d.rearrange("p (k t) -> p k t", k=8),
            in_=xkvT_d[:, :, cc * 1024:(cc + 1) * 1024]).then_inc(SEM["sIN"], 16))
        cload = b.bump("sIN", 16)
        for m in range(8):
            for tn, (wbuf, bias, scr) in enumerate(
                    ((wkb, bkc, kt_scr), (wvb, bvc, vt_scr))):
                u = {"m": m, "slot": slot}
                proj_unit(
                    u, wbuf, bias, 1.0,
                    lambda k, h, bu=buf: bu[:, k * 1024 + h * 512:
                                            k * 1024 + h * 512 + 512],
                    stage=stgs[slot % 2],
                    scr_ap=scr[:, m, c * 1024:(c + 1) * 1024],
                    pe_waits=(("sIN", max(wload, cload)),))
                slot += 1
        chunk_pe[c] = b.cnt["sPE"]

    if not do_p1b:
        b.barrier()
        return finish()

    b.emit("sp", lambda e: e.dma_start(
        out=xown.rearrange("p (k t) -> p k t", k=8),
        in_=xownT_d).then_inc(SEM["sIN"], 16))
    xo_in = b.bump("sIN", 16)

    for m in range(8):
        for h2 in range(2):
            u = {"m": m, "slot": slot}; slot += 1
            proj_unit(u, wqb, bqc, 0.125,
                      lambda k, h, hh=h2: xown[:, k * NT + hh * 1024 + h * 512:
                                               k * NT + hh * 1024 + h * 512 + 512],
                      dst_sb=QT[:, m * NT + h2 * 1024: m * NT + h2 * 1024 + 1024],
                      pe_waits=(("sIN", xo_in),))
    for jn, (wbuf, bias, scr) in enumerate(((wkb, bkc, ktd_scr),
                                            (wvb, bvc, vtd_scr))):
        for m in range(8):
            for h2 in range(2):
                u = {"m": m, "slot": slot}; slot += 1
                proj_unit(u, wbuf, bias, 1.0,
                          lambda k, h, hh=h2: xown[:, k * NT + hh * 1024 + h * 512:
                                                   k * NT + hh * 1024 + h * 512 + 512],
                          stage=stgs[slot % 2],
                          scr_ap=scr[:, m, h2 * 1024:(h2 + 1) * 1024],
                          pe_waits=(("sIN", xo_in),))
    scr_done = b.cnt["sSC"]

    b.wait("gp", "sSC", scr_done)
    pc_jobs = []
    for e_ in range(NEXP):
        for k in range(8):
            pc_jobs.append((w1_d[e_, k, :, :], w1f[e_, k, :, :]))
            pc_jobs.append((w2_d[e_, k, :, :], w2f[e_, k, :, :]))
        for blk in range(4):
            pc_jobs.append((
                w3_d[e_, blk].rearrange("a p n -> (a p) n"),
                w3f[e_, blk].rearrange("a p n -> (a p) n")))
    for src, dst in pc_jobs:
        b.emit("gp", lambda e, d=dst, ss=src: e.dma_start(
            out=d, in_=ss).then_inc(SEM["sGP"], 16))
        b.bump("sGP", 16)
    precast_done = b.cnt["sGP"]

    b.barrier()
    if stop_phase <= 1:
        return finish()

    def vs_ones(e):
        e.memset(Ve.rearrange("p (s c) -> p s c", c=65)[:, :, 64:65], 1.0)
        e.memset(Vo.rearrange("p (s c) -> p s c", c=65)[:, :, 64:65],
                 1.0).then_inc(SEM["sDVE"], 1)
    b.emit("dve", vs_ones)
    vones = b.bump("sDVE")

    b.wait("sp", "sSC", scr_done)

    kv_loaded = {}
    pair_visits_done = 0
    prev_norm_dve = 0
    prev_out_cp = 0

    pend = None

    def epi_stage1(pp, pnd):
        b.wait("pe", "sDVE", max(pp["rcp_v"], pnd))
        for half in range(2):
            def bc(e, o=pB[0:64, 1024 + half * 512:1024 + half * 512 + 512],
                   x=den16[:, half * 512:half * 512 + 512]):
                return e.matmul(o, onesb[64:65, 0:64], x, start=True,
                                stop=True, skip_group_check=True)
            b.emit("pe", bc)
        bc_v = b.mark("pe", "sPE")
        b.wait("dve", "sPE", bc_v)
        if pp["he"] == 0:
            def nrm(e, hp=pp["hp"], coff=pp["coff"]):
                e.tensor_mul(
                    attno[0:64, hp * NT + coff:hp * NT + coff + 1024],
                    outs[0:64, 0:1024],
                    pB[0:64, 1024:2048]).then_inc(SEM["sDVE"], 1)
            b.emit("dve", nrm)
            return b.bump("sDVE")
        def nrm(e):
            e.tensor_mul(tmpo[0:64, :], outs[0:64, 0:1024],
                         pB[0:64, 1024:2048]).then_inc(SEM["sDVE"], 1)
        b.emit("dve", nrm)
        return b.bump("sDVE")

    def epi_stage2(pp, nrm_v):
        b.wait("pe", "sDVE", nrm_v)
        for half in range(2):
            def sh(e, o=pB[64:128, 1024 + half * 512:1024 + half * 512 + 512],
                   x=tmpo[0:64, half * 512:half * 512 + 512]):
                return e.matmul(o, ids[0:64, 0:64], x, start=True,
                                stop=True, skip_group_check=True)
            b.emit("pe", sh)
        sh_v = b.mark("pe", "sPE")
        b.wait("dve", "sPE", sh_v)
        def cpo(e, hp=pp["hp"], coff=pp["coff"]):
            e.tensor_copy(
                attno[64:128, hp * NT + coff:hp * NT + coff + 1024],
                pB[64:128, 1024:2048]).then_inc(SEM["sDVE"], 1)
        b.emit("dve", cpo)
        return b.bump("sDVE")

    for hp in range(8):
        vb = VTb[hp % 2]
        if hp == 0:
            b.emit("sp", lambda e, d=vb: e.dma_start(
                out=d, in_=vt_scr[:, 0, :]).then_inc(SEM["sSC"], 16))
            kv_loaded[("v", 0)] = b.bump("sSC", 16)
        vtb_ld = kv_loaded[("v", hp)]
        if hp > 0:
            b.wait("sp", "sPE", pair_visits_done)
        for dst, src in ((KTb, kt_scr[:, hp, :]), (KTd, ktd_scr[:, hp, :]),
                         (VTd, vtd_scr[:, hp, :])):
            b.emit("sp", lambda e, d=dst, s=src: e.dma_start(
                out=d, in_=s).then_inc(SEM["sSC"], 16))
            b.bump("sSC", 16)
        k_ld = b.cnt["sSC"]

        grp_evac = {}
        for g in range((NSLOT + 7) // 8):
            lo = g * 8
            hi = min(lo + 8, NSLOT)
            bank = g % 4
            b.wait("pe", "sSC", max(vtb_ld, k_ld))
            if g >= 4:
                b.wait("pe", "sDVE", grp_evac[g - 4])
            for u in range(lo, hi):
                if u < NBB:
                    src = vb[:, u * 128:(u + 1) * 128]
                else:
                    du = u - NBB
                    src = VTd[:, du * 128:(du + 1) * 128]
                def tp(e, o=pB16[:, bank * 1024 + (u - lo) * 128:
                                 bank * 1024 + (u - lo + 1) * 128], s=src):
                    return e.transpose(o, s, ids[:])
                b.emit("pe", tp)
            vtp = b.mark("pe", "sPE")
            b.wait("dve", "sPE", vtp)
            b.wait("dve", "sDVE", vones)
            n = hi - lo
            def ev(e, n=n, lo=lo, bank=bank):
                sr = pB16[:, bank * 1024:bank * 1024 + n * 128].rearrange(
                    "p (s c) -> p s c", c=128)
                ve = Ve[:, lo * 65:(lo + n) * 65].rearrange(
                    "p (s c) -> p s c", c=65)
                vo = Vo[:, lo * 65:(lo + n) * 65].rearrange(
                    "p (s c) -> p s c", c=65)
                e.tensor_copy(ve[:, :, 0:64], sr[:, :, 0:64])
                e.tensor_copy(vo[:, :, 0:64], sr[:, :, 64:128]).then_inc(
                    SEM["sDVE"], 1)
            b.emit("dve", ev)
            grp_evac[g] = b.bump("sDVE")
        vslots_done = b.cnt["sDVE"]

        if hp < 7:
            nxt = VTb[(hp + 1) % 2]
            b.wait("sp", "sPE", b.cnt["sPE"])
            b.emit("sp", lambda e, d=nxt, h=hp + 1: e.dma_start(
                out=d, in_=vt_scr[:, h, :]).then_inc(SEM["sSC"], 16))
            kv_loaded[("v", hp + 1)] = b.bump("sSC", 16)

        for he, base, vs in ((0, 0, Ve), (1, 64, Vo)):
            for ci, (coff, nbulk, mask_base, dslot) in enumerate(
                    ((0, NBA, 0, NBB), (1024, NBB, NBA, NBB + NDIAG))):
                exp_v = {}
                av_v = {}
                qk_m = {}
                mask_v = {}
                nvis = nbulk + NDIAG
                first_av = True
                seg_pe = b.cnt["sPE"]
                seg_act = b.cnt["sACT"]

                def visit_info(v):
                    diag = v >= nbulk
                    dd = v - nbulk
                    if diag:
                        return (True, dd, dslot + dd, KTd, coff + dd * 128)
                    return (False, dd, v, KTb, v * 128)

                def emit_qk(v):
                    diag, dd, slot_v, kt_ap, kt_off = visit_info(v)
                    par = v % 2
                    ps = pA[:, par * 1024:(par + 1) * 1024]
                    b.wait("pe", "sACT", seg_act + v - 1 if v >= 2 else seg_act)
                    for half in range(2):
                        def qk(e, o=ps[:, half * 512:half * 512 + 512],
                               w=kt_ap[base:base + 64, kt_off:kt_off + 128],
                               x=QT[base:base + 64,
                                    hp * NT + coff + half * 512:
                                    hp * NT + coff + half * 512 + 512]):
                            return e.matmul(o, w, x, start=True, stop=True,
                                            skip_group_check=True)
                        b.emit("pe", qk)
                    qk_m[v] = b.mark("pe", "sPE")
                    if diag:
                        b.wait("dve", "sPE", qk_m[v])
                        def msk(e, dd=dd, ps=ps):
                            i = None
                            if dd > 0:
                                e.tensor_scalar_add(ps[:, 0:dd * 128],
                                                    ps[:, 0:dd * 128], NEG)
                            i = e.tensor_add(
                                ps[:, dd * 128:(dd + 1) * 128],
                                ps[:, dd * 128:(dd + 1) * 128], tri[:])
                            i.then_inc(SEM["sDVE"], 1)
                        b.emit("dve", msk)
                        mask_v[v] = b.bump("sDVE")

                emit_qk(0)
                for v in range(nvis):
                    diag, dd, slot_v, kt_ap, kt_off = visit_info(v)
                    par = v % 2
                    ps = pA[:, par * 1024:(par + 1) * 1024]
                    if v + 1 < nvis:
                        emit_qk(v + 1)
                    atb = at[v % 2]
                    if v in mask_v:
                        b.wait("act", "sDVE", mask_v[v])
                    else:
                        b.wait("act", "sPE", qk_m[v])
                    b.wait("act", "sPE", av_v[v - 2] if v >= 2 else seg_pe)
                    bias_arg = 0.0 if diag else \
                        amask[:, mask_base + v:mask_base + v + 1]
                    def ex(e, o=atb[:, 0:1024], ss=ps, bb=bias_arg):
                        e.activation(o, ss, AF.Exp, bias=bb,
                                     scale=1.0).then_inc(SEM["sACT"], 1)
                    b.emit("act", ex)
                    exp_v[v] = b.bump("sACT")
                    assert exp_v[v] == seg_act + v + 1
                    if first_av:
                        b.wait("pe", "sDVE", max(vslots_done, prev_out_cp))
                        first_av = False
                    b.wait("pe", "sACT", exp_v[v])
                    for half in range(2):
                        def av(e, o=pB[0:65, half * 512:half * 512 + 512],
                               w=vs[:, slot_v * 65:(slot_v + 1) * 65],
                               x=atb[:, half * 512:half * 512 + 512],
                               st=(v == 0), so=(v == nvis - 1)):
                            return e.matmul(o, w, x, start=st, stop=so,
                                            skip_group_check=True)
                        b.emit("pe", av)
                    def warm(e, o=pA[:, par * 1024:par * 1024 + 512]):
                        return e.matmul(o, ids[:], QT[0:128, 0:512],
                                        start=True, stop=True,
                                        skip_group_check=True)
                    b.emit("pe", warm)
                    av_v[v] = b.mark("pe", "sPE")
                    if pend is not None:
                        if pend["stage"] == 0 and v >= 1:
                            nv = epi_stage1(pend, prev_norm_dve)
                            if pend["he"] == 0:
                                prev_norm_dve = nv
                                pend = None
                            else:
                                pend["stage"] = 1
                                pend["nrm_v"] = nv
                        elif pend["stage"] == 1 and v >= 3:
                            prev_norm_dve = epi_stage2(pend, pend["nrm_v"])
                            pend = None

                avl = av_v[nvis - 1]
                b.wait("dve", "sPE", avl)
                def cp(e):
                    with nc.allow_low_precision(reason="f16 1/den operand"):
                        e.reciprocal(den16, pB[64:65, 0:1024])
                    e.tensor_copy(outs[0:65, 0:1024],
                                  pB[0:65, 0:1024]).then_inc(SEM["sDVE"], 1)
                b.emit("dve", cp)
                prev_out_cp = b.bump("sDVE")
                pend = dict(stage=0, rcp_v=prev_out_cp, he=he, hp=hp,
                            coff=coff)
        if pend is not None:
            nv = epi_stage1(pend, prev_norm_dve)
            if pend["he"] == 0:
                prev_norm_dve = nv
            else:
                prev_norm_dve = epi_stage2(pend, nv)
            pend = None
        b.wait("pe", "sDVE", prev_norm_dve)
        pair_visits_done = b.cnt["sPE"]

    attno_done = b.cnt["sDVE"]
    b.barrier()
    if stop_phase <= 2:
        return finish()
    for _i in range(4):
        psl_free[_i] = 0

    b.emit("sp", lambda e: e.dma_start(out=wob, in_=wo_d).then_inc(
        SEM["sIN"], 16))
    b.bump("sIN", 16)
    b.emit("sp", lambda e: e.dma_start(
        out=xown2.rearrange("p (k t) -> p k t", k=8),
        in_=xownT_d).then_inc(SEM["sIN"], 16))
    wo_ld = b.bump("sIN", 16)
    b.wait("gp", "sBAR", b.cnt["sBAR"])
    b.emit("gp", lambda e: e.dma_start(
        out=wrs, in_=wr32_d.rearrange("p k e -> p (k e)")).then_inc(
        SEM["sGP"], 16))
    wrs_ld = b.bump("sGP", 16)

    slot3 = 0
    for m in range(8):
        for h2 in range(2):
            sl = slot3 % 4; slot3 += 1
            ps = PSL[sl]
            b.wait("pe", "sIN", wo_ld)
            if psl_free[sl]:
                b.wait("pe", "sDVE", psl_free[sl])
            for half in range(2):
                for k in range(8):
                    def mm(e, o=ps[:, half * 512:(half + 1) * 512],
                           w=wob[:, k * D + m * 128:k * D + (m + 1) * 128],
                           x=attno[:, k * NT + h2 * 1024 + half * 512:
                                   k * NT + h2 * 1024 + half * 512 + 512],
                           st=(k == 0), so=(k == 7)):
                        return e.matmul(o, w, x, start=st, stop=so,
                                 skip_group_check=True)
                    b.emit("pe", mm)
            vmm = b.mark("pe", "sPE")
            b.wait("dve", "sPE", vmm)
            def x1f(e, s=ps, m=m, h2=h2):
                e.scalar_tensor_tensor(
                    x2[:, m * NT + h2 * 1024:m * NT + h2 * 1024 + 1024],
                    s, boc[:, m:m + 1],
                    xown2[:, m * NT + h2 * 1024:m * NT + h2 * 1024 + 1024],
                    OP.add, OP.add).then_inc(SEM["sDVE"], 1)
            b.emit("dve", x1f)
            psl_free[sl] = b.bump("sDVE")
    x1_done = b.cnt["sDVE"]

    for hf in range(2):
        hoff = hf * 1024
        b.wait("pe", "sDVE", x1_done)
        sq_mm = {}
        for m in range(8):
            sqb = sq[m % 2]
            b.wait("act", "sDVE", x1_done)
            if m >= 2:
                b.wait("act", "sPE", sq_mm[m - 2])
            def sqf(e, m=m, dd=sqb, hoff=hoff):
                e.activation(dd[:, 0:1024], x2[:, m * NT + hoff:
                                               m * NT + hoff + 1024],
                             AF.Square, bias=0.0, scale=1.0).then_inc(
                    SEM["sACT"], 1)
            b.emit("act", sqf)
            sq_done = b.bump("sACT")
            for half in range(2):
                def mmm(e, o=pA[0:1, half * 512:(half + 1) * 512],
                        x=x2[:, m * NT + hoff + half * 512:
                             m * NT + hoff + half * 512 + 512],
                        st=(m == 0), so=(m == 7)):
                    return e.matmul(o, ones_col[:], x, start=st, stop=so,
                             skip_group_check=True)
                b.emit("pe", mmm)
            b.wait("pe", "sACT", sq_done)
            for half in range(2):
                def mms(e, o=pA[0:1, 1024 + half * 512:1024 + (half + 1) * 512],
                        x=sqb[:, half * 512:half * 512 + 512],
                        st=(m == 0), so=(m == 7)):
                    return e.matmul(o, ones_col[:], x, start=st, stop=so,
                             skip_group_check=True)
                b.emit("pe", mms)
            sq_mm[m] = b.mark("pe", "sPE")
        stats_done = sq_mm[7]
        b.wait("dve", "sPE", stats_done)
        rg = b.dvechain([
            lambda e: e.tensor_scalar_mul(rows[:, 0:1024], pA[0:1, 0:1024],
                                          1.0 / D),
            lambda e: e.tensor_scalar_mul(rows[:, 1024:2048],
                                          pA[0:1, 1024:2048], 1.0 / D),
            lambda e: e.tensor_mul(rows[:, 2048:3072], rows[:, 0:1024],
                                   rows[:, 0:1024]),
            lambda e: e.tensor_sub(rows[:, 1024:2048], rows[:, 1024:2048],
                                   rows[:, 2048:3072]),
            lambda e: e.tensor_scalar_add(rows[:, 1024:2048],
                                          rows[:, 1024:2048], EPS)])
        b.wait("act", "sDVE", rg)
        def rsd(e):
            e.activation(rows[:, 2048:3072], rows[:, 1024:2048], AF.Sqrt,
                         bias=0.0, scale=1.0).then_inc(SEM["sACT"], 1)
        b.emit("act", rsd)
        rsv = b.bump("sACT")
        b.wait("dve", "sACT", rsv)
        r16 = b.dvechain([
            lambda e: e.reciprocal(rows[:, 2048:3072], rows[:, 2048:3072]),
            lambda e: e.tensor_copy(rows16[:, 0:1024], rows[:, 0:1024]),
            lambda e: e.tensor_copy(rows16[:, 1024:2048],
                                    rows[:, 2048:3072])])
        b.wait("pe", "sDVE", r16)
        for half in range(2):
            def bm(e, h=half):
                return e.matmul(pA[:, h * 512:(h + 1) * 512], ones_row[0:1, 0:128],
                         rows16[:, h * 512:h * 512 + 512], start=True,
                         stop=True, skip_group_check=True)
            b.emit("pe", bm)
            def br_(e, h=half):
                return e.matmul(pB[:, h * 512:(h + 1) * 512], ones_row[0:1, 0:128],
                         rows16[:, 1024 + h * 512:1024 + h * 512 + 512],
                         start=True, stop=True, skip_group_check=True)
            b.emit("pe", br_)
        bcast_v = b.mark("pe", "sPE")
        b.wait("dve", "sPE", bcast_v)
        for m in range(8):
            sl_ = x2[:, m * NT + hoff:m * NT + hoff + 1024]
            x1_done = b.dvechain([
                lambda e, sl_=sl_: e.tensor_sub(sl_, sl_, pA[0:128, 0:1024]),
                lambda e, sl_=sl_: e.tensor_mul(sl_, sl_, pB[0:128, 0:1024])])
    x2_done = x1_done
    if debug:
        b.wait("sp", "sDVE", x2_done)
        b.emit("sp", lambda e: e.dma_start(out=dbg_x2, in_=A[2][:, :]).then_inc(
            SEM["sOUT"], 16))
        b.bump("sOUT", 16)
        b.wait("sp", "sDVE", attno_done)
        b.emit("sp", lambda e: e.dma_start(out=dbg_at, in_=A[1][:, :]).then_inc(
            SEM["sOUT"], 16))
        b.bump("sOUT", 16)

    b.wait("pe", "sDVE", x2_done)
    b.wait("pe", "sGP", wrs_ld)
    for half in range(4):
        for k in range(8):
            def mm(e, h=half, k=k):
                return e.matmul(pA[0:8, h * 512:(h + 1) * 512],
                         wrs[:, k * 8:(k + 1) * 8],
                         x2[:, k * NT + h * 512:k * NT + h * 512 + 512],
                         start=(k == 0), stop=False, skip_group_check=True)
            b.emit("pe", mm)
        def mmb(e, h=half):
            return e.matmul(pA[0:8, h * 512:(h + 1) * 512], brr[:],
                     ones_row[0:1, 0:512], start=False, stop=True,
                     skip_group_check=True)
        b.emit("pe", mmb)
    log_mm = b.mark("pe", "sPE")
    b.wait("act", "sPE", log_mm)
    def logev(e):
        e.activation(logT, pA[0:8, 0:2048], AF.Copy, bias=0.0,
                     scale=1.0).then_inc(SEM["sACT"], 1)
    b.emit("act", logev)
    log_v = b.bump("sACT")
    b.wait("pe", "sACT", log_v)
    for t in range(16):
        def tp(e, t=t):
            return e.transpose(pB[:, t * 8:(t + 1) * 8],
                        logT[:, t * 128:(t + 1) * 128], ids32[0:8, 0:8])
        b.emit("pe", tp)
    ltp_v = b.mark("pe", "sPE")
    b.wait("dve", "sPE", ltp_v)
    def lcp(e):
        e.tensor_copy(ltm, pB[:, 0:128]).then_inc(SEM["sDVE"], 1)
    b.emit("dve", lcp)
    ltm_v = b.bump("sDVE")
    b.wait("dve", "sDVE", ltm_v)

    for t in range(16):
        L = ltm[:, t * 8:(t + 1) * 8]
        C = ctm[:, t * 8:(t + 1) * 8]
        m1 = t2a[:, 0:1]; m2 = t2a[:, 1:2]
        s1c = t2a[:, 2:3]; s2c = t2a[:, 3:4]
        dm = t2a[:, 4:5]; w1_ = t2a[:, 5:6]; w2_ = t2a[:, 6:7]
        oh1 = t2b; oh2 = t2c
        dv = b.dvechain([
            lambda e, L=L, m1=m1: e.tensor_reduce(
                m1, L, mybir.AxisListType.X, OP.max),
            lambda e, L=L, m1=m1, oh1=oh1: e.tensor_single_scalar(
                oh1, L, m1, OP.is_ge),
            lambda e, oh1=oh1, oh2=oh2: e.tensor_mul(oh2, oh1, rank8[:]),
            lambda e, oh2=oh2, s1c=s1c: e.tensor_reduce(
                s1c, oh2, mybir.AxisListType.X, OP.max),
            lambda e, oh1=oh1, oh2=oh2, s1c=s1c: e.tensor_single_scalar(
                oh1, oh2, s1c, OP.is_equal),
            lambda e, C=C, oh1=oh1, L=L: e.scalar_tensor_tensor(
                C, oh1, -1e30, L, OP.mult, OP.add),
            lambda e, C=C, m2=m2: e.tensor_reduce(
                m2, C, mybir.AxisListType.X, OP.max),
            lambda e, C=C, m2=m2, oh2=oh2: e.tensor_single_scalar(
                oh2, C, m2, OP.is_ge),
            lambda e, oh2=oh2: e.tensor_mul(oh2, oh2, rank8[:]),
            lambda e, oh2=oh2, s2c=s2c: e.tensor_reduce(
                s2c, oh2, mybir.AxisListType.X, OP.max),
            lambda e, oh2=oh2, s2c=s2c: e.tensor_single_scalar(
                oh2, oh2, s2c, OP.is_equal),
            lambda e, m1=m1, m2=m2, dm=dm: e.tensor_sub(dm, m1, m2)])
        b.wait("act", "sDVE", dv)
        def a1(e, dm=dm, w1_=w1_):
            e.activation(w1_, dm, AF.Sigmoid, bias=0.0,
                         scale=1.0).then_inc(SEM["sACT"], 1)
        b.emit("act", a1)
        av_ = b.bump("sACT")
        b.wait("dve", "sACT", av_)
        b.dvechain([
            lambda e, w1_=w1_, w2_=w2_: e.tensor_scalar(
                w2_, w1_, -1.0, 1.0, OP.mult, OP.add),
            lambda e, C=C, oh1=oh1, w1_=w1_: e.tensor_scalar_mul(C, oh1, w1_),
            lambda e, C=C, oh2=oh2, w2_=w2_: e.scalar_tensor_tensor(
                C, oh2, w2_, C, OP.mult, OP.add)])
    cw_tm_v = b.cnt["sDVE"]
    b.wait("pe", "sDVE", cw_tm_v)
    for t in range(16):
        def tp(e, t=t):
            return e.transpose(pA[0:8, t * 128:(t + 1) * 128],
                        ctm[:, t * 8:(t + 1) * 8], ids32[:])
        b.emit("pe", tp)
    cwtp_v = b.mark("pe", "sPE")
    b.wait("dve", "sPE", cwtp_v)
    def cwc(e):
        e.tensor_copy(cwT, pA[0:8, 0:2048]).then_inc(SEM["sDVE"], 1)
    b.emit("dve", cwc)
    cwT_v = b.bump("sDVE")
    if debug:
        b.wait("sp", "sDVE", cwT_v)
        b.emit("sp", lambda e: e.dma_start(out=dbg_cw, in_=cwT).then_inc(
            SEM["sOUT"], 16))
        b.bump("sOUT", 16)
        b.emit("sp", lambda e: e.dma_start(out=dbg_lt, in_=logT).then_inc(
            SEM["sOUT"], 16))
        b.bump("sOUT", 16)
        b.emit("sp", lambda e: e.dma_start(out=dbg_tm[:, 0:128],
                                           in_=ltm).then_inc(SEM["sOUT"], 16))
        b.bump("sOUT", 16)
        b.emit("sp", lambda e: e.dma_start(out=dbg_tm[:, 128:256],
                                           in_=ctm).then_inc(SEM["sOUT"], 16))
        b.bump("sOUT", 16)

    b.barrier()
    if stop_phase <= 3:
        return finish()

    b.wait("sp", "sGP", precast_done)

    w12_loaded = {}
    w12_freed = {}
    w3_loaded = {}
    w3_freed = {}

    def emit_w12_dma(p, e_, mg):
        par = mg % 2
        if (par,) in w12_freed:
            b.wait("sp", "sPE", w12_freed[(par,)])
        for wd, wbuf in ((w1f, w1b[par]), (w2f, w2b[par])):
            for k in range(8):
                def f(e, d=wbuf[:, k * 512:(k + 1) * 512],
                      s=wd[e_, k, :, mg * 512:(mg + 1) * 512]):
                    e.dma_start(out=d, in_=s).then_inc(SEM["sW"], 16)
                b.emit("sp", f)
                b.bump("sW", 16)
        w12_loaded[(p, e_, mg)] = b.cnt["sW"]

    def emit_w3_dma(p, e_, blk, kk):
        par = kk % 2
        if (par,) in w3_freed:
            b.wait("sp", "sPE", w3_freed[(par,)])
        def f(e, d=w3b[par][:, 0:512], s=w3f[e_, blk, kk, :, :]):
            e.dma_start(out=d, in_=s).then_inc(SEM["sW"], 16)
        b.emit("sp", f)
        b.bump("sW", 16)
        w3_loaded[(p, e_, blk, kk)] = b.cnt["sW"]

    def ybank(tc, ml):
        arena = pA if tc == 0 else pB
        return arena[:, ml * 512:(ml + 1) * 512]

    cwbs = [cwb, cwb2]
    free_ar = {0: 0, 1: 0}

    def emit_cwb(pp, ee):
        tbb = pp * 1024
        for half in range(2):
            def bcw(e, h=half, ee=ee, tbb=tbb):
                return e.matmul(pA[:, h * 512:(h + 1) * 512],
                                e8sel[:, ee * 128:(ee + 1) * 128],
                                cwT[:, tbb + h * 512:tbb + h * 512 + 512],
                                start=True, stop=True, skip_group_check=True)
            b.emit("pe", bcw)
        cmm = b.mark("pe", "sPE")
        b.wait("dve", "sPE", cmm)
        def cwbe(e, d=cwbs[(pp * 8 + ee) % 2]):
            e.tensor_copy(d, pA[:, 0:1024]).then_inc(SEM["sDVE"], 1)
        b.emit("dve", cwbe)
        return b.bump("sDVE")

    for p in range(2):
        tb = p * 1024

        for grp in range(2):
            b.wait("pe", "sDVE", cwT_v)
            for ml in range(4):
                m = grp * 4 + ml
                for tc in range(2):
                    b.wait("pe", "sDVE", free_ar[tc])
                    def mm(e, o=ybank(tc, ml), m=m, tc=tc, tb=tb):
                        return e.matmul(o, b3s[:, m * 128:(m + 1) * 128],
                                 cwT[:, tb + tc * 512:tb + tc * 512 + 512],
                                 start=True, stop=True, skip_group_check=True)
                    b.emit("pe", mm)
            vinit_mm = b.mark("pe", "sPE")
            b.wait("dve", "sPE", vinit_mm)
            def vev(e, grp=grp):
                i = None
                for ml in range(4):
                    m = grp * 4 + ml
                    for tc in range(2):
                        i = e.tensor_copy(
                            vacc[:, m * 1024 + tc * 512:
                                 m * 1024 + tc * 512 + 512],
                            ybank(tc, ml))
                i.then_inc(SEM["sDVE"], 1)
            b.emit("dve", vev)
            vv = b.bump("sDVE")
            free_ar[0] = vv
            free_ar[1] = vv

        if p == 0:
            b.wait("pe", "sDVE", free_ar[0])
            b.wait("pe", "sDVE", cwT_v)
            cwb_v = emit_cwb(0, 0)
            free_ar[0] = cwb_v

        for e_ in range(NEXP):
            mycwb = cwbs[(p * 8 + e_) % 2]
            if (p, e_, 0) not in w12_loaded:
                emit_w12_dma(p, e_, 0)
            if (p, e_, 1) not in w12_loaded:
                emit_w12_dma(p, e_, 1)

            if e_ < NEXP - 1:
                nxt_pe = (p, e_ + 1)
            elif p == 0:
                nxt_pe = (1, 0)
            else:
                nxt_pe = None

            silu_v = {}
            mult_v = {}
            for m in range(32):
                mg = m // 4
                par = mg % 2
                ml = m % 4
                if ml == 0 and mg >= 1:
                    tgt = mg + 1
                    if tgt <= 7:
                        emit_w12_dma(p, e_, tgt)
                    elif nxt_pe is not None:
                        emit_w12_dma(nxt_pe[0], nxt_pe[1], 0)
                arena = pA if m % 2 == 0 else pB
                h1 = arena[:, 0:1024]
                h2 = arena[:, 1024:2048]
                if m >= 2:
                    b.wait("pe", "sACT", silu_v[m - 2])
                    b.wait("pe", "sDVE", mult_v[m - 2])
                elif m == 0:
                    b.wait("pe", "sDVE", cwb_v)
                    b.wait("pe", "sDVE", free_ar[0])
                else:
                    b.wait("pe", "sDVE", free_ar[1])
                b.wait("pe", "sW", w12_loaded[(p, e_, mg)])
                for k in range(8):
                    for half in range(2):
                        def mm(e, o=h1[:, half * 512:half * 512 + 512],
                               w=w1b[par][:, k * 512 + ml * 128:
                                          k * 512 + ml * 128 + 128],
                               x=x2[:, k * NT + tb + half * 512:
                                    k * NT + tb + half * 512 + 512],
                               st=(k == 0), so=(k == 7)):
                            return e.matmul(o, w, x, start=st, stop=so,
                                     skip_group_check=True)
                        b.emit("pe", mm)
                h1_mm = b.mark("pe", "sPE")
                for k in range(8):
                    for half in range(2):
                        def mm(e, o=h2[:, half * 512:half * 512 + 512],
                               w=w2b[par][:, k * 512 + ml * 128:
                                          k * 512 + ml * 128 + 128],
                               x=x2[:, k * NT + tb + half * 512:
                                    k * NT + tb + half * 512 + 512],
                               st=(k == 0), so=(k == 7)):
                            return e.matmul(o, w, x, start=st, stop=so,
                                     skip_group_check=True)
                        b.emit("pe", mm)
                h2_mm = b.mark("pe", "sPE")
                if ml == 3:
                    w12_freed[(par,)] = h2_mm
                b.wait("act", "sPE", h1_mm)
                if m >= 2:
                    b.wait("act", "sDVE", mult_v[m - 2])
                def sil(e, s=h1, dd=s1b[m % 2], e_=e_, m=m):
                    e.activation(dd, s, AF.Silu,
                                 bias=b1s[:, e_ * 32 + m:e_ * 32 + m + 1],
                                 scale=1.0).then_inc(SEM["sACT"], 1)
                b.emit("act", sil)
                silu_v[m] = b.bump("sACT")
                b.wait("dve", "sPE", h2_mm)
                b.wait("dve", "sACT", silu_v[m])
                def mlt(e, s=h2, dd=hb(m), sc=s1b[m % 2], e_=e_, m=m):
                    e.scalar_tensor_tensor(
                        dd, s, b2s[:, e_ * 32 + m:e_ * 32 + m + 1], sc,
                        OP.add, OP.mult).then_inc(SEM["sDVE"], 1)
                b.emit("dve", mlt)
                mult_v[m] = b.bump("sDVE")

            if nxt_pe is not None and (nxt_pe[0], nxt_pe[1], 1) not in w12_loaded:
                emit_w12_dma(nxt_pe[0], nxt_pe[1], 1)
            yev_blk = {}
            for blk in range(4):
                ar = pA if blk % 2 == 0 else pB
                emit_w3_dma(p, e_, blk, 0)
                for kk in range(16):
                    if kk + 1 < 16:
                        emit_w3_dma(p, e_, blk, kk + 1)
                    b.wait("pe", "sW", w3_loaded[(p, e_, blk, kk)])
                    par = kk % 2
                    if kk == 0:
                        if blk == 0:
                            b.wait("pe", "sACT", silu_v[30])
                            b.wait("pe", "sDVE", mult_v[30])
                            b.wait("pe", "sDVE", free_ar[0])
                        elif blk == 1:
                            b.wait("pe", "sACT", silu_v[31])
                            b.wait("pe", "sDVE", mult_v[31])
                            b.wait("pe", "sDVE", free_ar[1])
                        else:
                            b.wait("pe", "sDVE", yev_blk[blk - 2])
                            if blk == 2:
                                b.wait("pe", "sDVE", cwb_v)
                    if blk == 0 and kk == 15:
                        b.wait("pe", "sDVE", mult_v[31])
                    for ksub in range(2):
                        k = kk * 2 + ksub
                        for tc in range(2):
                            for ml2 in range(2):
                                def mm(e, o=ar[:, (ml2 * 2 + tc) * 512:
                                               (ml2 * 2 + tc + 1) * 512],
                                       w=w3b[par][:, ksub * 256 + ml2 * 128:
                                                  ksub * 256 + ml2 * 128 + 128],
                                       x=hb(k)[:, tc * 512:tc * 512 + 512],
                                       st=(k == 0), so=(k == 31)):
                                    return e.matmul(o, w, x, start=st, stop=so,
                                             skip_group_check=True)
                                b.emit("pe", mm)
                    kmark = b.mark("pe", "sPE")
                    w3_freed[(par,)] = kmark
                y_mm = b.cnt["sPE"]
                ncwb_v = None
                if blk == 1 and nxt_pe is not None:
                    b.wait("pe", "sDVE", yev_blk[0])
                    ncwb_v = emit_cwb(nxt_pe[0], nxt_pe[1])
                b.wait("dve", "sPE", y_mm)
                chain_v = None
                for ml2 in range(2):
                    for tc in range(2):
                        m = blk * 2 + ml2
                        vsl = vacc[:, m * 1024 + tc * 512:
                                   m * 1024 + tc * 512 + 512]
                        chain_v = b.dvechain([
                            lambda e, o=ar[:, (ml2 * 2 + tc) * 512:
                                           (ml2 * 2 + tc + 1) * 512],
                                   tc=tc, cb=mycwb: e.tensor_mul(
                                yst, o, cb[:, tc * 512:tc * 512 + 512]),
                            lambda e, vsl=vsl: e.tensor_add(vsl, vsl, yst)])
                yev_blk[blk] = chain_v
                free_ar[blk % 2] = chain_v
                if ncwb_v is not None:
                    cwb_v = ncwb_v

        for m in range(8):
            def resf(e, m=m, last=(m == 7), tb=tb):
                i = e.tensor_add(vacc[:, m * 1024:(m + 1) * 1024],
                                 vacc[:, m * 1024:(m + 1) * 1024],
                                 x2[:, m * NT + tb:m * NT + tb + 1024])
                if last:
                    i.then_inc(SEM["sDVE"], 1)
            b.emit("dve", resf)
        vres = b.bump("sDVE")
        b.wait("pe", "sDVE", vres)
        sqmm = {}
        for m in range(8):
            sqb = s1b[m % 2]
            b.wait("act", "sDVE", vres)
            if m >= 2:
                b.wait("act", "sPE", sqmm[m - 2])
            def sqf(e, m=m, dd=sqb):
                e.activation(dd, vacc[:, m * 1024:(m + 1) * 1024], AF.Square,
                             bias=0.0, scale=1.0).then_inc(SEM["sACT"], 1)
            b.emit("act", sqf)
            sqv = b.bump("sACT")
            for half in range(2):
                def mmm(e, o=pA[0:1, half * 512:(half + 1) * 512],
                        x=vacc[:, m * 1024 + half * 512:
                               m * 1024 + half * 512 + 512],
                        st=(m == 0), so=(m == 7)):
                    return e.matmul(o, ones_col[:], x, start=st, stop=so,
                             skip_group_check=True)
                b.emit("pe", mmm)
            b.wait("pe", "sACT", sqv)
            for half in range(2):
                def mms(e, o=pA[0:1, 1024 + half * 512:1024 + (half + 1) * 512],
                        x=sqb[:, half * 512:half * 512 + 512],
                        st=(m == 0), so=(m == 7)):
                    return e.matmul(o, ones_col[:], x, start=st, stop=so,
                             skip_group_check=True)
                b.emit("pe", mms)
            sqmm[m] = b.mark("pe", "sPE")
        st2 = sqmm[7]
        b.wait("dve", "sPE", st2)
        rg = b.dvechain([
            lambda e: e.tensor_scalar_mul(rows[:, 0:1024], pA[0:1, 0:1024],
                                          1.0 / D),
            lambda e: e.tensor_scalar_mul(rows[:, 1024:2048],
                                          pA[0:1, 1024:2048], 1.0 / D),
            lambda e: e.tensor_mul(rows[:, 2048:3072], rows[:, 0:1024],
                                   rows[:, 0:1024]),
            lambda e: e.tensor_sub(rows[:, 1024:2048], rows[:, 1024:2048],
                                   rows[:, 2048:3072]),
            lambda e: e.tensor_scalar_add(rows[:, 1024:2048],
                                          rows[:, 1024:2048], EPS)])
        b.wait("act", "sDVE", rg)
        def rsd(e):
            e.activation(rows[:, 2048:3072], rows[:, 1024:2048], AF.Sqrt,
                         bias=0.0, scale=1.0).then_inc(SEM["sACT"], 1)
        b.emit("act", rsd)
        rsv = b.bump("sACT")
        b.wait("dve", "sACT", rsv)
        r16 = b.dvechain([
            lambda e: e.reciprocal(rows[:, 2048:3072], rows[:, 2048:3072]),
            lambda e: e.tensor_copy(rows16[:, 0:1024], rows[:, 0:1024]),
            lambda e: e.tensor_copy(rows16[:, 1024:2048],
                                    rows[:, 2048:3072])])
        b.wait("pe", "sDVE", r16)
        for half in range(2):
            def bm(e, h=half):
                return e.matmul(pA[:, h * 512:(h + 1) * 512], ones_row[0:1, 0:128],
                         rows16[:, h * 512:h * 512 + 512], start=True,
                         stop=True, skip_group_check=True)
            b.emit("pe", bm)
            def br_(e, h=half):
                return e.matmul(pB[:, h * 512:(h + 1) * 512], ones_row[0:1, 0:128],
                         rows16[:, 1024 + h * 512:1024 + h * 512 + 512],
                         start=True, stop=True, skip_group_check=True)
            b.emit("pe", br_)
        bc2 = b.mark("pe", "sPE")
        out_dma = {}
        for m in range(8):
            ob = ost[m % 2]
            b.wait("dve", "sPE", bc2)
            if m >= 2:
                b.wait("dve", "sOUT", out_dma[m - 2])
            elif p > 0:
                b.wait("dve", "sOUT", prev_pass_out)
            fv = b.dvechain([
                lambda e, m=m, dd=ob: e.tensor_sub(
                    dd, vacc[:, m * 1024:(m + 1) * 1024], pA[0:128, 0:1024]),
                lambda e, dd=ob: e.tensor_mul(dd, dd, pB[0:128, 0:1024])])
            b.wait("sp", "sDVE", fv)
            b.emit("sp", lambda e, dd=out_d[:, m, tb:tb + 1024], s=ob:
                   e.dma_start(out=dd, in_=s).then_inc(SEM["sOUT"], 16))
            out_dma[m] = b.bump("sOUT", 16)
        lastfin = b.cnt["sDVE"]
        prev_pass_out = b.cnt["sOUT"]
        free_ar[0] = lastfin
        free_ar[1] = lastfin

    return finish()


def _finish(b, nc):
    SEM = b.sem
    b.wait("sp", "sOUT", b.cnt["sOUT"])
    b.wait("sp", "sSC", b.cnt["sSC"])
    b.wait("sp", "sW", b.cnt["sW"])
    b.wait("sp", "sIN", b.cnt["sIN"])
    b.wait("gp", "sGP", b.cnt["sGP"])
    b.barrier()

    with nc.Block() as block:
        @block.tensor
        def _(e):
            for f in b.q["pe"]:
                f(e)

        @block.scalar
        def _(e):
            for f in b.q["act"]:
                f(e)

        @block.vector
        def _(e):
            for f in b.q["dve"]:
                f(e)

        @block.sync
        def _(e):
            for f in b.q["sp"]:
                f(e)

        @block.gpsimd
        def _(e):
            for f in b.q["gp"]:
                f(e)

    b.stack.close()
    return nc


def _prep_inputs(full):
    hid = np.asarray(full["L0_attn_hidden_states"])
    wq = np.asarray(full["L0_attn_wq"]); bq = np.asarray(full["L0_attn_bq"])
    wk = np.asarray(full["L0_attn_wk"]); bk = np.asarray(full["L0_attn_bk"])
    wv = np.asarray(full["L0_attn_wv"]); bv = np.asarray(full["L0_attn_bv"])
    wo = np.asarray(full["L0_attn_wo"]); bo = np.asarray(full["L0_attn_bo"])
    Wr = np.asarray(full["L0_ffn_Wr"]); br = np.asarray(full["L0_ffn_br"])
    W1 = np.asarray(full["L0_ffn_W1"]); b1 = np.asarray(full["L0_ffn_b1"])
    W2 = np.asarray(full["L0_ffn_W2"]); b2 = np.asarray(full["L0_ffn_b2"])
    W3 = np.asarray(full["L0_ffn_W3"]); b3 = np.asarray(full["L0_ffn_b3"])

    def wT_sb(w):
        wt = np.ascontiguousarray(w.astype(np.float16).T)
        return np.ascontiguousarray(
            wt.reshape(8, 128, D).transpose(1, 0, 2).reshape(128, 8 * D))

    def bias_cols(x):
        return np.ascontiguousarray(x.astype(np.float16).reshape(8, 128).T)

    def fm(xT):
        return np.ascontiguousarray(xT.reshape(8, 128, -1).transpose(1, 0, 2))

    tri_np = np.where(np.arange(128)[None, :] < np.arange(128)[:, None],
                      np.float32(NEG), np.float32(0.0)).astype(np.float32)
    rank8 = np.broadcast_to(np.arange(8, 0, -1, dtype=np.float32),
                            (128, 8)).copy()
    shared = {
        "wqT": wT_sb(wq), "wkT": wT_sb(wk), "wvT": wT_sb(wv), "woT": wT_sb(wo),
        "bq": bq.astype(np.float16).reshape(1, D),
        "bk": bk.astype(np.float16).reshape(1, D),
        "bv": bv.astype(np.float16).reshape(1, D),
        "bo": bias_cols(bo),
        "brr": br.astype(np.float16).reshape(1, 8),
        "ids16": np.eye(128, dtype=np.float16),
        "ids32": np.eye(128, dtype=np.float32),
        "tri": tri_np, "rank8": rank8,
        "wr32": np.ascontiguousarray(
            Wr.astype(np.float32).reshape(8, 128, 8).transpose(1, 0, 2)),
        "b1c": np.ascontiguousarray(
            b1.astype(np.float32).reshape(8, 32, 128).transpose(2, 0, 1)
            .reshape(128, 256)),
        "b2c": np.ascontiguousarray(
            b2.astype(np.float32).reshape(8, 32, 128).transpose(2, 0, 1)
            .reshape(128, 256)),
        "b3": b3.astype(np.float32),
        "e8sel": np.ascontiguousarray(
            np.kron(np.eye(8, dtype=np.float16),
                    np.ones((1, 128), np.float16))),
        "w1": np.ascontiguousarray(W1.astype(np.float32).reshape(8, 8, 128, DFF)),
        "w2": np.ascontiguousarray(W2.astype(np.float32).reshape(8, 8, 128, DFF)),
        "w3": np.ascontiguousarray(
            W3.astype(np.float32).reshape(8, 16, 2, 128, 4, 256)
            .transpose(0, 4, 1, 3, 2, 5).reshape(8, 4, 16, 128, 512)),
    }

    in_maps = []
    metas = []
    for c in range(8):
        bt = c // 4
        j = c % 4
        hT = np.ascontiguousarray(hid[bt].T)
        cA = hT[:, 1024 * j:1024 * (j + 1)]
        cB = hT[:, 1024 * (7 - j):1024 * (8 - j)]
        am = np.zeros((128, NBA + NBB), dtype=np.float32)
        for v in range(NBA):
            if v >= 8 * j:
                am[:, v] = NEG
        for v in range(NBB):
            if v >= 8 * (7 - j):
                am[:, NBA + v] = NEG
        m = dict(shared)
        m["xkvT"] = fm(hT)
        m["xownT"] = fm(np.ascontiguousarray(np.concatenate([cA, cB], axis=1)))
        m["amask"] = am
        in_maps.append(m)
        metas.append((bt, j))
    return in_maps, metas


_NC_CACHE = {}


def kernel(**inputs):
    in_maps, metas = _prep_inputs(inputs)
    if "nc" not in _NC_CACHE:
        _NC_CACHE["nc"] = build_program()
    nc = _NC_CACHE["nc"]
    res = run_bass_kernel_spmd(nc, in_maps, core_ids=list(range(8)))
    out = np.zeros((2, T, D), dtype=np.float32)
    for c, (bt, j) in enumerate(metas):
        y = res.results[c]["y"]
        yT = np.asarray(y).transpose(1, 0, 2).reshape(D, NT)
        out[bt, 1024 * j:1024 * (j + 1), :] = yT[:, 0:1024].T
        out[bt, 1024 * (7 - j):1024 * (8 - j), :] = yT[:, 1024:2048].T
    return out
